# version 15 (final)
# baseline (speedup 1.0000x reference)
import contextlib
import numpy as np
import ml_dtypes
import concourse.bass as bass
import concourse.mybir as mybir
from concourse.bass_utils import run_bass_kernel_spmd

F32 = mybir.dt.float32
BF16 = mybir.dt.bfloat16
AF = mybir.ActivationFunctionType
ALU = mybir.AluOpType
AX = mybir.AxisListType

B, L, D = 2, 4096, 2048
H, HKV, HD = 16, 4, 128
T, S = 1024, 4096
NQ, NKS, ND = 8, 32, 16
FE, NF = 1408, 11
NEXP = 12
ISCALE = float(1.0 / np.sqrt(128.0))

ARENA_N = 98304
R_V = 0
R_KT = 16384
R_XQ = 32768
R_OO = 49152
R_WB = 65536
R_H = 32768
R_GSB = 44032
R_CB = 45056
R_LGF = 34816
R_CSB = 38912
R_GSROW = 43008
R_WK = 65536
R_WV = 73728
R_XS = 81920
R_STG = 90112
R_XN = 94208
R_WOA = 65536
R_WOB = 81920
R_PT = 81920
R_R1 = 83968
R_RECB = 84992
R_RBSB = 85504
R_RECB2 = 86016
R_RBSB2 = 86528
R_R12 = 87040
R_WD = 65536
R_WGU = 88064
R_GATE = 96256


def build_program():
    nc = bass.Bass("TRN2", target_bir_lowering=False)
    dt = nc.dram_tensor
    xseq = dt("xseq", [S, D], F32, kind="ExternalInput").ap()
    wq_r = dt("wq_r", [4, 128, ND * 512], F32, kind="ExternalInput").ap()
    wk_r = dt("wk_r", [128, ND * 512], F32, kind="ExternalInput").ap()
    wv_r = dt("wv_r", [128, ND * 512], F32, kind="ExternalInput").ap()
    wo_r = dt("wo_r", [2, 128, 8 * 2048], F32, kind="ExternalInput").ap()
    wr_rr = dt("wr_r", [128, ND * 8], F32, kind="ExternalInput").ap()
    wsg_rr = dt("wsg_r", [128, ND * 1], F32, kind="ExternalInput").ap()
    wgu_r = dt("wgu_r", [NEXP, NF, 128, 2 * ND * 128], F32, kind="ExternalInput").ap()
    wd_r = dt("wd_r", [NEXP, 128, NF * 2048], F32, kind="ExternalInput").ap()
    cos_r = dt("cos_r", [128, NKS * 128], F32, kind="ExternalInput").ap()
    sin_r = dt("sin_r", [128, NKS * 128], F32, kind="ExternalInput").ap()
    identb = dt("identb", [128, 128], BF16, kind="ExternalInput").ap()
    identf = dt("identf", [128, 128], F32, kind="ExternalInput").ap()
    onesrow = dt("onesrow", [1, 128], BF16, kind="ExternalInput").ap()
    onescol = dt("onescol", [128, 1], BF16, kind="ExternalInput").ap()
    sel = dt("sel", [8, 8 * 128], BF16, kind="ExternalInput").ap()
    consts = dt("consts", [128, 2], F32, kind="ExternalInput").ap()
    out = dt("out", [T, D], F32, kind="ExternalOutput").ap()

    with contextlib.ExitStack() as stack:
        E = stack.enter_context
        ar = E(nc.sbuf_tensor("arena", [128, ARENA_N], BF16)).ap()
        stat = E(nc.sbuf_tensor("stat", [128, 64], F32)).ap()
        cst = E(nc.sbuf_tensor("cst", [128, 2], F32)).ap()
        idb = E(nc.sbuf_tensor("idb", [128, 128], BF16)).ap()
        idf = E(nc.sbuf_tensor("idf", [128, 128], F32)).ap()
        ones_r = E(nc.sbuf_tensor("ones_r", [1, 128], BF16)).ap()
        ones_c = E(nc.sbuf_tensor("ones_c", [128, 1], BF16)).ap()
        sel_sb = E(nc.sbuf_tensor("sel_sb", [8, 8 * 128], BF16)).ap()
        tcb = E(nc.sbuf_tensor("tcb", [128, 512], F32)).ap()
        tsb = E(nc.sbuf_tensor("tsb", [128, 512], F32)).ap()
        dacc = E(nc.sbuf_tensor("dacc", [128, 1024], F32)).ap()
        krotb = E(nc.sbuf_tensor("krotb", [128, 512], BF16)).ap()
        krotb2 = E(nc.sbuf_tensor("krotb2", [128, 512], BF16)).ap()
        lgtok = E(nc.sbuf_tensor("lgtok", [128, 64], F32)).ap()
        rtmp = E(nc.sbuf_tensor("rtmp", [128, 64], F32)).ap()
        rst = E(nc.sbuf_tensor("rst", [128, 64], F32)).ap()
        ctok = E(nc.sbuf_tensor("ctok", [128, 64], F32)).ap()
        cbf = tcb[0:8, :].bitcast(BF16)
        wrsb = E(nc.sbuf_tensor("wrsb", [128, ND * 8], BF16)).ap()
        wsgsb = E(nc.sbuf_tensor("wsgsb", [128, ND], BF16)).ap()
        ps = E(nc.psum_tensor("ps", [128, 4096], F32)).ap()
        sLD = E(nc.semaphore("sLD"))
        sWT = E(nc.semaphore("sWT"))
        sTT = E(nc.semaphore("sTT"))
        sVV = E(nc.semaphore("sVV"))
        sSS = E(nc.semaphore("sSS"))
        sST = E(nc.semaphore("sST"))
        sW6a = E(nc.semaphore("sW6a"))
        sW6b = E(nc.semaphore("sW6b"))
        sW6d = E(nc.semaphore("sW6d"))

        def bf(off, n):
            return ar[:, off:off + n]

        def f32v(off, nf32):
            return ar[:, off:off + 2 * nf32].bitcast(F32)

        def PB(b, w=1):
            return ps[:, 512 * b:512 * (b + w)]

        def PBH(b, w=1):
            return ps[:, 512 * b:512 * (b + w)].bitcast(BF16)

        nc.const_aps.register(F32, 1e-5, cst[:, 0:1])
        cosv = f32v(R_OO, 4096)
        sinv = f32v(R_OO + 8192, 4096)
        xsv = f32v(R_XS, 4096)
        x1v = f32v(R_V, 16384)

        nLD = nWT = nTT = nVV = nSS = 0
        wt_ev = {}

        bL, bT, bV, bS = nLD, nTT, nVV, nSS
        ev_x = lambda ks: bL + 16 * (ks + 1)

        t1_list = []
        for g in range(NKS + 2):
            if g < NKS:
                t1_list += [("trA", g), ("trB", g)]
            if 0 <= g - 1 < NKS:
                t1_list.append(("kv", g - 1))
            if 0 <= g - 2 < NKS:
                t1_list.append(("ktr", g - 2))
        evT1 = {k: bT + i + 1 for i, k in enumerate(t1_list)}
        v1_list = []
        for g in range(NKS + 3):
            if 0 <= g - 1 < NKS:
                v1_list += [("evA", g - 1), ("evB", g - 1)]
            if g < NKS:
                v1_list += [("bn", g), ("xn", g)]
            if 0 <= g - 2 < NKS:
                v1_list += [("vev", g - 2), ("krope", g - 2)]
            if 0 <= g - 3 < NKS:
                v1_list.append(("ktev", g - 3))
        evV1 = {k: bV + i + 1 for i, k in enumerate(v1_list)}
        ev_xn = lambda ks: evV1[("xn", ks)]
        ev_bn = lambda ks: evV1[("bn", ks)]
        ev_trB = lambda ks: evT1[("trB", ks)]
        ev_rstd = lambda ks: bS + ks + 1

        with nc.Block("p1a") as blk:
            @blk.sync
            def _(sy):
                nn = bL
                for ks in range(NKS):
                    if ks >= 2:
                        sy.wait_ge(sVV, ev_xn(ks - 2))
                    sy.wait_ge(sLD, nn)
                    sy.dma_start(out=xsv[:, (ks % 2) * 2048:(ks % 2) * 2048 + 2048],
                                 in_=xseq[ks * 128:(ks + 1) * 128, :]).then_inc(sLD, 16)
                    nn += 16

            @blk.gpsimd
            def _(g):
                pre = [(idb, identb[:, :]), (cst, consts[:, :]),
                       (cosv, cos_r[:, :]), (sinv, sin_r[:, :]),
                       (idf, identf[:, :]), (ones_r, onesrow[:, :]),
                       (ones_c, onescol[:, :]), (sel_sb, sel[:, :]),
                       (bf(R_WK, 8192), wk_r[:, :]), (bf(R_WV, 8192), wv_r[:, :])]
                nn = nWT
                for o_, i_ in pre:
                    g.wait_ge(sWT, nn)
                    g.dma_start(out=o_, in_=i_).then_inc(sWT, 16)
                    nn += 16
                g.wait_ge(sTT, evT1[("kv", NKS - 1)])
                g.wait_ge(sVV, evV1[("xn", NKS - 1)])
                for cb in range(4):
                    g.wait_ge(sWT, nn)
                    g.dma_start(out=bf(R_WB + cb * 8192, 8192),
                                in_=wq_r[cb, :, :]).then_inc(sWT, 16)
                    nn += 16

            nWT += 14 * 16
            wt_ev["idb"] = nWT - 224 + 16
            wt_ev["consts"] = nWT - 224 + 64
            wt_ev["wkv"] = nWT - 64
            wt_ev["wq"] = nWT

            @blk.tensor
            def _(t):
                t.wait_ge(sWT, wt_ev["idb"])
                first_kv = [True]
                for item in t1_list:
                    kind, g = item
                    if kind == "kv" and first_kv[0]:
                        t.wait_ge(sWT, wt_ev["wkv"])
                        first_kv[0] = False
                    if kind == "trA":
                        xn = bf(R_XN + (g % 2) * 2048, 2048)
                        t.wait_ge(sVV, ev_xn(g))
                        if g >= 1:
                            t.wait_ge(sVV, evV1[("evB", g - 1)])
                        for dc in range(8):
                            ins = t.transpose(PBH(0)[:, dc * 128:(dc + 1) * 128],
                                              xn[:, dc * 128:(dc + 1) * 128], idb)
                        ins.then_inc(sTT, 1)
                    elif kind == "trB":
                        xn = bf(R_XN + (g % 2) * 2048, 2048)
                        for dc in range(8, 16):
                            ins = t.transpose(PBH(1)[:, (dc - 8) * 128:(dc - 7) * 128],
                                              xn[:, dc * 128:(dc + 1) * 128], idb)
                        ins.then_inc(sTT, 1)
                    elif kind == "kv":
                        ks = g
                        own = ks < NQ
                        t.wait_ge(sVV, evV1[("evB", ks)])
                        if ks >= 2:
                            t.wait_ge(sVV, evV1[("krope", ks - 2)])
                            t.wait_ge(sVV, evV1[("vev", ks - 2)])
                        kb = 2 + (ks % 2) * 2
                        vb = 3 + (ks % 2) * 2
                        for dc in range(ND):
                            if own:
                                lhs = bf(R_XQ + dc * 1024 + ks * 128, 128)
                            else:
                                lhs = bf(R_STG + (ks % 2) * 2048 + dc * 128, 128)
                            t.matmul(PB(kb), lhs, bf(R_WK + dc * 512, 512),
                                     start=(dc == 0), stop=(dc == ND - 1))
                            ins = t.matmul(PB(vb), lhs, bf(R_WV + dc * 512, 512),
                                           start=(dc == 0), stop=(dc == ND - 1))
                        ins.then_inc(sTT, 1)
                    else:
                        ks = g
                        t.wait_ge(sVV, evV1[("krope", ks)])
                        if ks >= 2:
                            t.wait_ge(sVV, evV1[("ktev", ks - 2)])
                        ktb = 6 + (ks % 2)
                        kr = krotb if ks % 2 == 0 else krotb2
                        for j in range(4):
                            ins = t.transpose(PBH(ktb)[:, j * 128:(j + 1) * 128],
                                              kr[:, j * 128:(j + 1) * 128], idb)
                        ins.then_inc(sTT, 1)

            @blk.vector
            def _(v):
                v.wait_ge(sWT, wt_ev["consts"])
                for item in v1_list:
                    kind, g = item
                    if kind == "evA":
                        ks = g
                        v.wait_ge(sTT, evT1[("trA", ks)])
                        if ks < NQ:
                            dst = bf(R_XQ, 16384).rearrange("p (c q) -> p c q", c=16)[:, 0:8, ks * 128:(ks + 1) * 128]
                        else:
                            dst = bf(R_STG + (ks % 2) * 2048, 1024).rearrange("p (c q) -> p c q", c=8)
                        v.tensor_copy(dst, PBH(0).rearrange("p (c q) -> p c q", c=8)).then_inc(sVV, 1)
                    elif kind == "evB":
                        ks = g
                        v.wait_ge(sTT, evT1[("trB", ks)])
                        if ks < NQ:
                            dst = bf(R_XQ, 16384).rearrange("p (c q) -> p c q", c=16)[:, 8:16, ks * 128:(ks + 1) * 128]
                        else:
                            dst = bf(R_STG + (ks % 2) * 2048 + 1024, 1024).rearrange("p (c q) -> p c q", c=8)
                        v.tensor_copy(dst, PBH(1).rearrange("p (c q) -> p c q", c=8)).then_inc(sVV, 1)
                    elif kind == "bn":
                        ks = g
                        xt = xsv[:, (ks % 2) * 2048:(ks % 2) * 2048 + 2048]
                        sb = (ks % 2) * 32
                        v.wait_ge(sLD, ev_x(ks))
                        v.reduce_sum(stat[:, sb + 29:sb + 30], xt, axis=AX.X)
                        v.drain()
                        v.tensor_scalar(stat[:, sb + 24:sb + 25], stat[:, sb + 29:sb + 30],
                                        1.0 / 2048.0, None, ALU.mult).then_inc(sVV, 1)
                    elif kind == "xn":
                        ks = g
                        xt = xsv[:, (ks % 2) * 2048:(ks % 2) * 2048 + 2048]
                        xn = bf(R_XN + (ks % 2) * 2048, 2048)
                        sb = (ks % 2) * 32
                        v.wait_ge(sSS, ev_rstd(ks))
                        v.tensor_scalar(xn, xt, stat[:, sb + 24:sb + 25],
                                        stat[:, sb + 26:sb + 27],
                                        ALU.subtract, ALU.mult).then_inc(sVV, 1)
                    elif kind == "vev":
                        ks = g
                        v.wait_ge(sTT, evT1[("kv", ks)])
                        v.tensor_copy(bf(R_V + ks * 512, 512), PB(3 + (ks % 2) * 2)).then_inc(sVV, 1)
                    elif kind == "krope":
                        ks = g
                        kb = 2 + (ks % 2) * 2
                        if ks >= 2:
                            v.wait_ge(sTT, evT1[("ktr", ks - 2)])
                        cs = cosv[:, ks * 128:(ks + 1) * 128].rearrange("p (r d) -> p r d", r=1).broadcast_to([128, 4, 128])
                        sn = sinv[:, ks * 128:(ks + 1) * 128]
                        v.tensor_tensor(tcb.rearrange("p (r d) -> p r d", r=4),
                                        PB(kb).rearrange("p (r d) -> p r d", r=4), cs, ALU.mult)
                        for j in range(4):
                            v.tensor_tensor(tsb[:, j * 128 + 1:(j + 1) * 128],
                                            PB(kb)[:, j * 128:j * 128 + 127],
                                            sn[:, 1:128], ALU.mult)
                            v.tensor_tensor(tsb[:, j * 128:j * 128 + 1],
                                            PB(kb)[:, j * 128 + 127:(j + 1) * 128],
                                            sn[:, 0:1], ALU.mult)
                        v.drain()
                        kr = krotb if ks % 2 == 0 else krotb2
                        v.tensor_tensor(kr, tcb, tsb, ALU.add).then_inc(sVV, 1)
                    else:
                        ks = g
                        v.wait_ge(sTT, evT1[("ktr", ks)])
                        ktb = 6 + (ks % 2)
                        dst = bf(R_KT, 16384).rearrange("p (j k) -> p j k", j=4)[:, :, ks * 128:(ks + 1) * 128]
                        v.tensor_copy(dst, PBH(ktb)[:, 0:512].rearrange("p (j k) -> p j k", j=4)).then_inc(sVV, 1)

            @blk.scalar
            def _(s):
                s.wait_ge(sWT, wt_ev["consts"])
                for ks in range(NKS):
                    sb = (ks % 2) * 32
                    xt = xsv[:, (ks % 2) * 2048:(ks % 2) * 2048 + 2048]
                    xn = bf(R_XN + (ks % 2) * 2048, 2048)
                    s.wait_ge(sLD, ev_x(ks))
                    if ks >= 2:
                        s.wait_ge(sTT, ev_trB(ks - 2))
                    s.activation(xn, xt, AF.Square,
                                 accum_out=stat[:, sb + 28:sb + 29])
                    s.drain()
                    s.wait_ge(sVV, ev_bn(ks))
                    s.mul(stat[:, sb + 31:sb + 32], stat[:, sb + 24:sb + 25], -1.0)
                    s.drain()
                    s.mul(stat[:, sb + 27:sb + 28], stat[:, sb + 24:sb + 25],
                          stat[:, sb + 31:sb + 32])
                    s.drain()
                    s.activation(stat[:, sb + 30:sb + 31], stat[:, sb + 27:sb + 28],
                                 AF.Copy, bias=1e-5)
                    s.drain()
                    s.activation(stat[:, sb + 27:sb + 28], stat[:, sb + 28:sb + 29],
                                 AF.Ln, bias=stat[:, sb + 30:sb + 31], scale=1.0 / 2048.0)
                    s.activation(stat[:, sb + 26:sb + 27], stat[:, sb + 27:sb + 28],
                                 AF.Exp, scale=-0.5)
                    s.drain().then_inc(sSS, 1)

        nLD = bL + 16 * NKS
        nTT = bT + 4 * NKS
        nVV = bV + 7 * NKS
        nSS = bS + NKS

        bT, bV = nTT, nVV
        ev_q = lambda qs: bT + 5 * qs + 1
        ev_qtr = lambda qs, cb: bT + 5 * qs + 2 + cb
        ev_qrope = lambda qs, cb: bV + 8 * qs + 2 * cb + 1
        ev_qtev = lambda qs, cb: bV + 8 * qs + 2 * cb + 2

        with nc.Block("p1b") as blk:
            @blk.tensor
            def _(t):
                t.wait_ge(sWT, wt_ev["wq"])
                for qs in range(NQ):
                    for dc in range(ND):
                        lhs = bf(R_XQ + dc * 1024 + qs * 128, 128)
                        for cb in range(4):
                            ins = t.matmul(PB(cb), lhs, bf(R_WB + cb * 8192 + dc * 512, 512),
                                           start=(dc == 0), stop=(dc == ND - 1))
                    ins.then_inc(sTT, 1)
                    for cb in range(4):
                        t.wait_ge(sVV, ev_qrope(qs, cb))
                        ftr = 4 * qs + cb
                        if ftr >= 2:
                            qsp, cbp = divmod(ftr - 2, 4)
                            t.wait_ge(sVV, ev_qtev(qsp, cbp))
                        ktb = 6 + (ftr % 2)
                        for j in range(4):
                            ins = t.transpose(PBH(ktb)[:, j * 128:(j + 1) * 128],
                                              krotb[:, j * 128:(j + 1) * 128], idb)
                        ins.then_inc(sTT, 1)

            @blk.vector
            def _(v):
                for qs in range(NQ):
                    v.wait_ge(sTT, ev_q(qs))
                    cs = cosv[:, qs * 128:(qs + 1) * 128].rearrange("p (r d) -> p r d", r=1).broadcast_to([128, 4, 128])
                    sn = sinv[:, qs * 128:(qs + 1) * 128]
                    for cb in range(4):
                        qp = PB(cb)
                        v.tensor_tensor(tcb.rearrange("p (r d) -> p r d", r=4),
                                        qp.rearrange("p (r d) -> p r d", r=4), cs, ALU.mult)
                        for j in range(4):
                            v.tensor_tensor(tsb[:, j * 128 + 1:(j + 1) * 128],
                                            qp[:, j * 128:j * 128 + 127], sn[:, 1:128], ALU.mult)
                            v.tensor_tensor(tsb[:, j * 128:j * 128 + 1],
                                            qp[:, j * 128 + 127:(j + 1) * 128], sn[:, 0:1], ALU.mult)
                        v.drain()
                        v.tensor_tensor(krotb, tcb, tsb, ALU.add).then_inc(sVV, 1)
                        v.wait_ge(sTT, ev_qtr(qs, cb))
                        ktb = 6 + ((4 * qs + cb) % 2)
                        dst = bf(R_XQ, 16384).rearrange("p (hh q) -> p hh q", hh=16)[:, 4 * cb:4 * cb + 4, qs * 128:(qs + 1) * 128]
                        v.tensor_copy(dst, PBH(ktb)[:, 0:512].rearrange("p (j k) -> p j k", j=4)).then_inc(sVV, 1)

        nTT = bT + 5 * NQ
        nVV = bV + 8 * NQ

        bT, bV, bS = nTT, nVV, nSS
        t2_list = []
        for h in range(H):
            t2_list += [("s", h, 0), ("s", h, 1)]
            for kt in range(NKS):
                t2_list.append(("g", h, kt))
                if kt < 30:
                    t2_list.append(("s", h, kt + 2))
            t2_list += [("d", h), ("rb", h)]
        evT2 = {k: bT + i + 1 for i, k in enumerate(t2_list)}
        ev2_exp = lambda h, kt: bS + 33 * h + kt + 1
        ev2_recb = lambda h: bS + 33 * h + 33
        ev2_add = lambda h, kt: bV + 36 * h + kt + 1
        ev2_comb = lambda h: bV + 36 * h + 33
        ev2_rbe = lambda h: bV + 36 * h + 34
        ev2_on0 = lambda h: bV + 36 * h + 35
        ev2_on1 = lambda h: bV + 36 * h + 36

        def pt_buf2(n):
            return bf(R_WGU + (n % 4) * 1024, 1024)

        r1v = f32v(R_R1, 512)[0:1, :]
        r1v2 = f32v(R_R12, 512)[0:1, :]
        recb0 = bf(R_RECB, 512)[0:1, :]
        recb1 = bf(R_RECB2, 512)[0:1, :]
        rbsb0 = bf(R_RBSB, 512)
        rbsb1 = bf(R_RBSB2, 512)

        with nc.Block("p2") as blk:
            @blk.gpsimd
            def _(g):
                g.wait_ge(sWT, nWT)
                g.dma_start(out=bf(R_WOA, 16384), in_=wo_r[0, :, :]).then_inc(sWT, 16)

            nWT += 16
            wt_ev["woA"] = nWT

            @blk.tensor
            def _(t):
                for item in t2_list:
                    if item[0] == "s":
                        _, h, kt = item
                        j = h // 4
                        if kt >= 2:
                            t.wait_ge(sSS, ev2_exp(h, kt - 2))
                        elif h >= 1:
                            t.wait_ge(sSS, ev2_exp(h - 1, 30 + kt))
                        for qt in range(2):
                            ins = t.matmul(PB(2 * (kt % 2) + qt),
                                           bf(R_KT + j * 4096 + kt * 128, 128),
                                           bf(R_XQ + h * 1024 + qt * 512, 512),
                                           start=True, stop=True)
                        ins.then_inc(sTT, 1)
                    elif item[0] == "g":
                        _, h, kt = item
                        j = h // 4
                        n = h * 32 + kt
                        t.wait_ge(sSS, ev2_exp(h, kt))
                        if kt == 0 and h >= 1:
                            t.wait_ge(sVV, ev2_on1(h - 1))
                        for qt in range(2):
                            ins = t.matmul(PB(4 + qt), bf(R_V + kt * 512 + j * 128, 128),
                                           pt_buf2(n)[:, qt * 512:(qt + 1) * 512],
                                           start=(kt == 0), stop=(kt == 31))
                        ins.then_inc(sTT, 1)
                    elif item[0] == "d":
                        h = item[1]
                        t.wait_ge(sVV, ev2_comb(h))
                        if h >= 1:
                            t.wait_ge(sVV, ev2_rbe(h - 1))
                        combv = bf(R_PT, 1024)
                        for qt in range(2):
                            ins = t.matmul(ps[0:1, 512 * (6 + qt):512 * (6 + qt) + 512],
                                           ones_c, combv[:, qt * 512:(qt + 1) * 512],
                                           start=True, stop=True)
                        ins.then_inc(sTT, 1)
                    else:
                        h = item[1]
                        t.wait_ge(sSS, ev2_recb(h))
                        t.matmul(PB(6), ones_r, recb0, start=True, stop=True)
                        t.matmul(PB(7), ones_r, recb1, start=True, stop=True).then_inc(sTT, 1)

            @blk.scalar
            def _(s):
                for h in range(H):
                    for kt in range(NKS):
                        n = h * 32 + kt
                        s.wait_ge(sTT, evT2[("s", h, kt)])
                        if n >= 4:
                            hp, ktp = divmod(n - 4, 32)
                            s.wait_ge(sTT, evT2[("g", hp, ktp)])
                            s.wait_ge(sVV, ev2_add(hp, ktp))
                        bk0 = 2 * (kt % 2)
                        s.activation(pt_buf2(n), ps[:, 512 * bk0:512 * (bk0 + 2)],
                                     AF.Exp, scale=ISCALE).then_inc(sSS, 1)
                    s.wait_ge(sTT, evT2[("d", h)])
                    s.activation(r1v, ps[0:1, 512 * 6:512 * 6 + 512], AF.Ln)
                    s.activation(recb0, r1v, AF.Exp, scale=-1.0)
                    s.activation(r1v2, ps[0:1, 512 * 7:512 * 7 + 512], AF.Ln)
                    s.activation(recb1, r1v2, AF.Exp, scale=-1.0)
                    s.drain().then_inc(sSS, 1)

            @blk.vector
            def _(v):
                for h in range(H):
                    for kt in range(NKS):
                        n = h * 32 + kt
                        v.wait_ge(sSS, ev2_exp(h, kt))
                        if kt == 0:
                            v.tensor_copy(dacc[:, :], pt_buf2(n)).then_inc(sVV, 1)
                        else:
                            v.tensor_tensor(dacc[:, :], dacc[:, :], pt_buf2(n),
                                            ALU.add).then_inc(sVV, 1)
                    v.tensor_copy(bf(R_PT, 1024), dacc[:, :]).then_inc(sVV, 1)
                    v.wait_ge(sTT, evT2[("rb", h)])
                    v.tensor_copy(rbsb0, PB(6))
                    v.drain()
                    v.tensor_copy(rbsb1, PB(7))
                    v.drain().then_inc(sVV, 1)
                    v.tensor_tensor(bf(R_OO + h * 1024, 512),
                                    PB(4), rbsb0, ALU.mult).then_inc(sVV, 1)
                    v.tensor_tensor(bf(R_OO + h * 1024 + 512, 512),
                                    PB(5), rbsb1, ALU.mult).then_inc(sVV, 1)

        nTT = bT + len(t2_list)
        nSS = bS + 33 * H
        nVV = bV + 36 * H

        bL, bT, bV = nLD, nTT, nVV
        ev_p3 = lambda p, qs: bT + p * NQ + qs + 1
        ev_p3add = lambda p, qs: bV + p * NQ + qs + 1
        ev_xown = lambda qs: bL + 16 * (qs + 1)

        with nc.Block("p3") as blk:
            @blk.sync
            def _(sy):
                for qs in range(NQ):
                    sy.wait_ge(sLD, bL + 16 * qs)
                    sy.dma_start(out=x1v[:, qs * 2048:(qs + 1) * 2048],
                                 in_=xseq[qs * 128:(qs + 1) * 128, :]).then_inc(sLD, 16)

            @blk.gpsimd
            def _(g):
                g.wait_ge(sWT, wt_ev["woA"])
                g.dma_start(out=bf(R_WOB, 16384), in_=wo_r[1, :, :]).then_inc(sWT, 16)

            nWT += 16
            wt_ev["woB"] = nWT

            @blk.tensor
            def _(t):
                t.wait_ge(sWT, wt_ev["woA"])
                for p in range(2):
                    if p == 1:
                        t.wait_ge(sWT, wt_ev["woB"])
                    for qs in range(NQ):
                        bk = (qs % 2) * 4
                        gidx = p * NQ + qs
                        if gidx >= 2:
                            pp, qq = divmod(gidx - 2, NQ)
                            t.wait_ge(sVV, ev_p3add(pp, qq))
                        for hh in range(8):
                            lhs = bf(R_OO + (p * 8 + hh) * 1024 + qs * 128, 128)
                            for doc in range(4):
                                ins = t.matmul(PB(bk + doc), lhs,
                                               bf((R_WOA if p == 0 else R_WOB) + hh * 2048 + doc * 512, 512),
                                               start=(hh == 0), stop=(hh == 7))
                        ins.then_inc(sTT, 1)

            @blk.vector
            def _(v):
                for p in range(2):
                    for qs in range(NQ):
                        bk = (qs % 2) * 4
                        v.wait_ge(sTT, ev_p3(p, qs))
                        if p == 0:
                            v.wait_ge(sLD, ev_xown(qs))
                        x1s = x1v[:, qs * 2048:(qs + 1) * 2048]
                        v.tensor_tensor(x1s, x1s, PB(bk, 4), ALU.add).then_inc(sVV, 1)

        nLD = bL + 16 * NQ
        nTT = bT + 2 * NQ
        nVV = bV + 2 * NQ

        bT, bV, bS = nTT, nVV, nSS
        ev4_bn = lambda qs: bV + 4 * qs + 1
        ev4_xn = lambda qs: bV + 4 * qs + 2
        ev4_evA = lambda qs: bV + 4 * qs + 3
        ev4_evB = lambda qs: bV + 4 * qs + 4
        ev4_rstd = lambda qs: bS + qs + 1
        ev4_trA = lambda qs: bT + 2 * qs + 1
        ev4_trB = lambda qs: bT + 2 * qs + 2

        def xnb(qs):
            return bf(R_WGU + (qs % 2) * 2048, 2048)

        with nc.Block("p4") as blk:
            @blk.gpsimd
            def _(g):
                g.dma_start(out=bf(R_WD, NF * 2048), in_=wd_r[0, :, :]).then_inc(sW6d, 16)

            wt_ev["wd0"] = 16

            @blk.tensor
            def _(t):
                for qs in range(NQ):
                    t.wait_ge(sVV, ev4_xn(qs))
                    if qs >= 1:
                        t.wait_ge(sVV, ev4_evA(qs - 1))
                    for dc in range(8):
                        ins = t.transpose(PBH(0)[:, dc * 128:(dc + 1) * 128],
                                          xnb(qs)[:, dc * 128:(dc + 1) * 128], idb)
                    ins.then_inc(sTT, 1)
                    if qs >= 1:
                        t.wait_ge(sVV, ev4_evB(qs - 1))
                    for dc in range(8, 16):
                        ins = t.transpose(PBH(1)[:, (dc - 8) * 128:(dc - 7) * 128],
                                          xnb(qs)[:, dc * 128:(dc + 1) * 128], idb)
                    ins.then_inc(sTT, 1)

            @blk.vector
            def _(v):
                for qs in range(NQ):
                    x1s = x1v[:, qs * 2048:(qs + 1) * 2048]
                    sb = (qs % 2) * 32
                    v.reduce_sum(stat[:, sb + 29:sb + 30], x1s, axis=AX.X)
                    v.drain()
                    v.tensor_scalar(stat[:, sb + 24:sb + 25], stat[:, sb + 29:sb + 30],
                                    1.0 / 2048.0, None, ALU.mult).then_inc(sVV, 1)
                    v.wait_ge(sSS, ev4_rstd(qs))
                    v.tensor_scalar(xnb(qs), x1s, stat[:, sb + 24:sb + 25],
                                    stat[:, sb + 26:sb + 27],
                                    ALU.subtract, ALU.mult).then_inc(sVV, 1)
                    v.wait_ge(sTT, ev4_trA(qs))
                    dst = bf(R_OO, 16384).rearrange("p (c q) -> p c q", c=16)[:, 0:8, qs * 128:(qs + 1) * 128]
                    v.tensor_copy(dst, PBH(0).rearrange("p (c q) -> p c q", c=8)).then_inc(sVV, 1)
                    v.wait_ge(sTT, ev4_trB(qs))
                    dst = bf(R_OO, 16384).rearrange("p (c q) -> p c q", c=16)[:, 8:16, qs * 128:(qs + 1) * 128]
                    v.tensor_copy(dst, PBH(1).rearrange("p (c q) -> p c q", c=8)).then_inc(sVV, 1)

            @blk.scalar
            def _(s):
                for qs in range(NQ):
                    sb = (qs % 2) * 32
                    x1s = x1v[:, qs * 2048:(qs + 1) * 2048]
                    if qs >= 2:
                        s.wait_ge(sTT, ev4_trB(qs - 2))
                    s.activation(xnb(qs), x1s, AF.Square,
                                 accum_out=stat[:, sb + 28:sb + 29])
                    s.drain()
                    s.wait_ge(sVV, ev4_bn(qs))
                    s.mul(stat[:, sb + 31:sb + 32], stat[:, sb + 24:sb + 25], -1.0)
                    s.drain()
                    s.mul(stat[:, sb + 27:sb + 28], stat[:, sb + 24:sb + 25],
                          stat[:, sb + 31:sb + 32])
                    s.drain()
                    s.activation(stat[:, sb + 30:sb + 31], stat[:, sb + 27:sb + 28],
                                 AF.Copy, bias=1e-5)
                    s.drain()
                    s.activation(stat[:, sb + 27:sb + 28], stat[:, sb + 28:sb + 29],
                                 AF.Ln, bias=stat[:, sb + 30:sb + 31], scale=1.0 / 2048.0)
                    s.activation(stat[:, sb + 26:sb + 27], stat[:, sb + 27:sb + 28],
                                 AF.Exp, scale=-0.5)
                    s.drain().then_inc(sSS, 1)

        nTT = bT + 2 * NQ
        nVV = bV + 4 * NQ
        nSS = bS + NQ

        bT, bV, bS = nTT, nVV, nSS
        lgF = ar[0:8, R_LGF:R_LGF + 2048].bitcast(F32)
        csbF = ar[0:8, R_CSB:R_CSB + 2048].bitcast(F32)
        gsrow = ar[0:1, 46080:46080 + 2048].bitcast(F32)
        gsrowb = ar[0:1, 48128:48128 + 1024]
        e5_lg = bT + 1
        e5_sig = bT + 2
        e5_lgt = bT + 3
        e5_ct = bT + 4
        e5_gsb = bT + 5
        e5_lge = bV + 1
        e5_neg = bV + 2
        e5_cu = bV + 3
        e5_ctok = bV + 4
        e5_csbe = bV + 5
        e5_cbf = bV + 6
        e5_gsrb = bV + 7
        e5_gsbe = bV + 8
        e5_exp = bS + 1
        e5_rcs = bS + 2
        e5_sg = bS + 3

        with nc.Block("p5") as blk:
            @blk.gpsimd
            def _(g):
                nn = nWT
                for o_, i_ in [(wrsb, wr_rr[:, :]), (wsgsb, wsg_rr[:, :])]:
                    g.wait_ge(sWT, nn)
                    g.dma_start(out=o_, in_=i_).then_inc(sWT, 16)
                    nn += 16
                g.dma_start(out=bf(R_WGU, 4096), in_=wgu_r[0, 0, :, :]).then_inc(sW6a, 16)
                g.dma_start(out=bf(R_WGU + 4096, 4096), in_=wgu_r[0, 1, :, :]).then_inc(sW6b, 16)

            nWT += 32
            wt_ev["wrsg"] = nWT
            wt_ev[("wgu", 0)] = 16
            wt_ev[("wgu", 1)] = 16

            @blk.tensor
            def _(t):
                t.wait_ge(sWT, wt_ev["wrsg"])
                for dc in range(ND):
                    for th in range(2):
                        ins = t.matmul(ps[0:8, th * 512:(th + 1) * 512],
                                       wrsb[:, dc * 8:(dc + 1) * 8],
                                       bf(R_OO + dc * 1024 + th * 512, 512),
                                       start=(dc == 0), stop=(dc == ND - 1))
                ins.then_inc(sTT, 1)
                for dc in range(ND):
                    for th in range(2):
                        ins = t.matmul(ps[0:1, 1024 + th * 512:1024 + (th + 1) * 512],
                                       wsgsb[:, dc:dc + 1],
                                       bf(R_OO + dc * 1024 + th * 512, 512),
                                       start=(dc == 0), stop=(dc == ND - 1))
                ins.then_inc(sTT, 1)
                t.wait_ge(sVV, e5_lge)
                for ts in range(8):
                    ins = t.transpose(ps[:, 2048 + ts * 8:2048 + ts * 8 + 8],
                                      lgF[:, ts * 128:(ts + 1) * 128], idf[0:8, 0:8])
                ins.then_inc(sTT, 1)
                t.wait_ge(sVV, e5_ctok)
                for ts in range(8):
                    ins = t.transpose(ps[0:8, 2560 + ts * 128:2560 + (ts + 1) * 128],
                                      ctok[:, ts * 8:(ts + 1) * 8], idf)
                ins.then_inc(sTT, 1)
                t.wait_ge(sVV, e5_gsrb)
                for th in range(2):
                    ins = t.matmul(PB(th), ones_r, gsrowb[:, th * 512:(th + 1) * 512],
                                   start=True, stop=True)
                ins.then_inc(sTT, 1)

            @blk.vector
            def _(v):
                v.wait_ge(sTT, e5_lg)
                v.tensor_copy(lgF, ps[0:8, 0:1024]).then_inc(sVV, 1)
                v.wait_ge(sTT, e5_lgt)
                v.tensor_copy(lgtok, ps[:, 2048:2112])
                v.drain()
                for ts in range(8):
                    c = lgtok[:, ts * 8:(ts + 1) * 8]
                    v.reduce_max(rst[:, ts * 8:ts * 8 + 1], c, axis=AX.X)
                v.drain()
                for ts in range(8):
                    ins = v.tensor_scalar(rst[:, ts * 8 + 1:ts * 8 + 2],
                                          rst[:, ts * 8:ts * 8 + 1], -1.0, None, ALU.mult)
                v.drain().then_inc(sVV, 1)
                v.wait_ge(sSS, e5_exp)
                for ts in range(8):
                    e = rtmp[:, ts * 8:(ts + 1) * 8]
                    v.reduce_sum(rst[:, ts * 8 + 2:ts * 8 + 3], e, axis=AX.X)
                    v.reduce_max(rst[:, ts * 8 + 3:ts * 8 + 4], e, axis=AX.X)
                    v.drain()
                    v.scalar_tensor_tensor(lgtok[:, ts * 8:(ts + 1) * 8], e,
                                           rst[:, ts * 8 + 3:ts * 8 + 4], e,
                                           ALU.is_lt, ALU.mult)
                    v.drain()
                    v.reduce_max(rst[:, ts * 8 + 4:ts * 8 + 5],
                                 lgtok[:, ts * 8:(ts + 1) * 8], axis=AX.X)
                    v.drain()
                    ins = v.scalar_tensor_tensor(ctok[:, ts * 8:(ts + 1) * 8], e,
                                                 rst[:, ts * 8 + 4:ts * 8 + 5], e,
                                                 ALU.is_ge, ALU.mult)
                v.drain().then_inc(sVV, 1)
                v.wait_ge(sSS, e5_rcs)
                for ts in range(8):
                    ins = v.tensor_scalar(ctok[:, ts * 8:(ts + 1) * 8],
                                          ctok[:, ts * 8:(ts + 1) * 8],
                                          rst[:, ts * 8 + 5:ts * 8 + 6], None, ALU.mult)
                ins.then_inc(sVV, 1)
                v.wait_ge(sTT, e5_ct)
                v.tensor_copy(csbF, ps[0:8, 2560:3584]).then_inc(sVV, 1)
                v.drain()
                v.tensor_copy(cbf, csbF)
                v.drain().then_inc(sVV, 1)
                v.wait_ge(sSS, e5_sg)
                v.tensor_copy(gsrowb, gsrow).then_inc(sVV, 1)
                v.wait_ge(sTT, e5_gsb)
                v.tensor_copy(bf(R_GSB, 1024), PBH(0, 2)[:, 0:1024] if False else PB(0, 2)).then_inc(sVV, 1)

            @blk.scalar
            def _(s):
                s.wait_ge(sVV, e5_neg)
                for ts in range(8):
                    ins = s.activation(rtmp[:, ts * 8:(ts + 1) * 8],
                                       lgtok[:, ts * 8:(ts + 1) * 8], AF.Exp,
                                       bias=rst[:, ts * 8 + 1:ts * 8 + 2])
                ins.then_inc(sSS, 1)
                s.wait_ge(sVV, e5_cu)
                for ts in range(8):
                    s.activation(rst[:, ts * 8 + 6:ts * 8 + 7],
                                 rst[:, ts * 8 + 2:ts * 8 + 3], AF.Ln)
                    ins = s.activation(rst[:, ts * 8 + 5:ts * 8 + 6],
                                       rst[:, ts * 8 + 6:ts * 8 + 7], AF.Exp, scale=-1.0)
                ins.then_inc(sSS, 1)
                s.wait_ge(sTT, e5_sig)
                s.activation(gsrow, ps[0:1, 1024:2048], AF.Sigmoid).then_inc(sSS, 1)

        nTT = bT + 5
        nVV = bV + 8
        nSS = bS + 3

        bT, bV, bS = nTT, nVV, nSS

        t_order = []
        for i in range(NEXP):
            if i < 8:
                t_order.append(("cb", i))
            t_order.append(("gate", i, 0))
            t_order.append(("up", i, 0))
            if i >= 1:
                for dn in range(16):
                    t_order.append(("down", i - 1, dn))
            for fc in range(1, NF):
                t_order.append(("gate", i, fc))
                t_order.append(("up", i, fc))
        for dn in range(16):
            t_order.append(("down", NEXP - 1, dn))
        ev6 = {k: bT + n + 1 for n, k in enumerate(t_order)}

        v_order = []
        for i in range(NEXP):
            if i < 8:
                v_order.append(("cbev", i))
            if i >= 1:
                for dn in range(16):
                    v_order.append(("dadd", i - 1, dn))
            for fc in range(NF):
                v_order.append(("h", i, fc))
        for dn in range(16):
            v_order.append(("dadd", NEXP - 1, dn))
        ev6v = {k: bV + n + 1 for n, k in enumerate(v_order)}

        def ev6_silu(i, fc):
            return bS + 11 * i + fc + 1

        with nc.Block("p6") as blk:
            @blk.gpsimd
            def _(g):
                w6 = [16, 16]
                w6d = 16
                for i in range(NEXP):
                    for fc in range(NF):
                        n = i * NF + fc
                        if n >= 2:
                            ip, fcp = divmod(n - 2, NF)
                            sem = sW6a if n % 2 == 0 else sW6b
                            g.wait_ge(sTT, ev6[("up", ip, fcp)])
                            g.wait_ge(sem, w6[n % 2])
                            g.dma_start(out=bf(R_WGU + (n % 2) * 4096, 4096),
                                        in_=wgu_r[i, fc, :, :]).then_inc(sem, 16)
                            w6[n % 2] += 16
                            wt_ev[("wgu", n)] = w6[n % 2]
                        if fc == 2 and i >= 1:
                            g.wait_ge(sTT, ev6[("down", i - 1, 15)])
                            g.wait_ge(sW6d, w6d)
                            g.dma_start(out=bf(R_WD, NF * 2048),
                                        in_=wd_r[i, :, :]).then_inc(sW6d, 16)
                            w6d += 16
                            wt_ev[("wd", i)] = w6d

            @blk.tensor
            def _(t):
                for item in t_order:
                    if item[0] == "cb":
                        i = item[1]
                        if i >= 1:
                            t.wait_ge(sSS, ev6_silu(i - 1, NF - 1))
                        for th in range(2):
                            ins = t.matmul(PB(th), sel_sb[:, i * 128:(i + 1) * 128],
                                           cbf[:, th * 512:(th + 1) * 512],
                                           start=True, stop=True)
                        ins.then_inc(sTT, 1)
                    elif item[0] == "gate":
                        _, i, fc = item
                        n = i * NF + fc
                        t.wait_ge(sW6a if n % 2 == 0 else sW6b, wt_ev[("wgu", n if n >= 2 else fc)])
                        if fc == 0:
                            if i < 8:
                                t.wait_ge(sVV, ev6v[("cbev", i)])
                            elif i >= 1:
                                t.wait_ge(sSS, ev6_silu(i - 1, NF - 1))
                        else:
                            t.wait_ge(sSS, ev6_silu(i, fc - 1))
                        for dc in range(ND):
                            for th in range(2):
                                ins = t.matmul(PB(th), bf(R_WGU + (n % 2) * 4096 + dc * 128, 128),
                                               bf(R_OO + dc * 1024 + th * 512, 512),
                                               start=(dc == 0), stop=(dc == ND - 1))
                        ins.then_inc(sTT, 1)
                    elif item[0] == "up":
                        _, i, fc = item
                        n = i * NF + fc
                        if fc >= 1:
                            t.wait_ge(sVV, ev6v[("h", i, fc - 1)])
                        elif i >= 1:
                            t.wait_ge(sVV, ev6v[("h", i - 1, NF - 1)])
                        for dc in range(ND):
                            for th in range(2):
                                ins = t.matmul(PB(2 + th), bf(R_WGU + (n % 2) * 4096 + 2048 + dc * 128, 128),
                                               bf(R_OO + dc * 1024 + th * 512, 512),
                                               start=(dc == 0), stop=(dc == ND - 1))
                        ins.then_inc(sTT, 1)
                    else:
                        _, i, dn = item
                        ts, hf = divmod(dn, 2)
                        gidx = i * 16 + dn
                        bk = 4 + 2 * (dn % 2)
                        if dn == 0:
                            t.wait_ge(sW6d, wt_ev[("wd", i)] if i >= 1 else wt_ev["wd0"])
                            t.wait_ge(sVV, ev6v[("h", i, NF - 1)])
                        if gidx >= 2:
                            ip, dnp = divmod(gidx - 2, 16)
                            t.wait_ge(sVV, ev6v[("dadd", ip, dnp)])
                        for fc in range(NF):
                            lhs = bf(R_H + fc * 1024 + ts * 128, 128)
                            t.matmul(PB(bk), lhs,
                                     bf(R_WD + fc * 2048 + hf * 1024, 512),
                                     start=(fc == 0), stop=(fc == NF - 1))
                            ins = t.matmul(PB(bk + 1), lhs,
                                           bf(R_WD + fc * 2048 + hf * 1024 + 512, 512),
                                           start=(fc == 0), stop=(fc == NF - 1))
                        ins.then_inc(sTT, 1)

            @blk.vector
            def _(v):
                for item in v_order:
                    if item[0] == "cbev":
                        i = item[1]
                        v.wait_ge(sTT, ev6[("cb", i)])
                        v.tensor_copy(bf(R_CB, 1024), PB(0, 2))
                        v.drain().then_inc(sVV, 1)
                    elif item[0] == "dadd":
                        _, i, dn = item
                        ts, hf = divmod(dn, 2)
                        bk = 4 + 2 * (dn % 2)
                        v.wait_ge(sTT, ev6[("down", i, dn)])
                        x1s = x1v[:, ts * 2048 + hf * 1024:ts * 2048 + (hf + 1) * 1024]
                        v.tensor_tensor(x1s, x1s, PB(bk, 2), ALU.add).then_inc(sVV, 1)
                    else:
                        _, i, fc = item
                        scaleb = bf(R_CB, 1024) if i < 8 else bf(R_GSB, 1024)
                        v.wait_ge(sTT, ev6[("up", i, fc)])
                        v.wait_ge(sSS, ev6_silu(i, fc))
                        hslice = bf(R_H + fc * 1024, 1024)
                        v.tensor_tensor(hslice, PB(2, 2), scaleb, ALU.mult)
                        v.drain()
                        v.tensor_tensor(hslice, hslice, bf(R_GATE, 1024), ALU.mult).then_inc(sVV, 1)

            @blk.scalar
            def _(s):
                for i in range(NEXP):
                    for fc in range(NF):
                        s.wait_ge(sTT, ev6[("gate", i, fc)])
                        if fc >= 1:
                            s.wait_ge(sVV, ev6v[("h", i, fc - 1)])
                        elif i >= 1:
                            s.wait_ge(sVV, ev6v[("h", i - 1, NF - 1)])
                        s.activation(bf(R_GATE, 1024), PB(0, 2), AF.Silu).then_inc(sSS, 1)

        nTT = bT + len(t_order)
        nVV = bV + len(v_order)
        nSS = bS + 11 * NEXP

        with nc.Block("p7") as blk:
            @blk.sync
            def _(sy):
                for qs in range(NQ):
                    sy.dma_start(out=out[qs * 128:(qs + 1) * 128, :],
                                 in_=x1v[:, qs * 2048:(qs + 1) * 2048]).then_inc(sST, 16)
                sy.wait_ge(sST, 16 * NQ)

    return nc


_NC_CACHE = [None]


def _prep_weights(inputs):
    f = lambda a: np.ascontiguousarray(a, dtype=np.float32)
    Wq = np.asarray(inputs["L0_attn_Wq"], np.float32)
    Wk = np.asarray(inputs["L0_attn_Wk"], np.float32)
    Wv = np.asarray(inputs["L0_attn_Wv"], np.float32)
    Wo = np.asarray(inputs["L0_attn_Wo"], np.float32)
    Wr = np.asarray(inputs["L0_ffn_Wr"], np.float32)
    Wsg = np.asarray(inputs["L0_ffn_Wsg"], np.float32)
    Wg = np.asarray(inputs["L0_ffn_Wg"], np.float32)
    Wu = np.asarray(inputs["L0_ffn_Wu"], np.float32)
    Wd = np.asarray(inputs["L0_ffn_Wd"], np.float32)
    Sg = np.asarray(inputs["L0_ffn_Sg"], np.float32)
    Su = np.asarray(inputs["L0_ffn_Su"], np.float32)
    Sd = np.asarray(inputs["L0_ffn_Sd"], np.float32)

    wq_r = f(Wq.reshape(16, 128, 4, 512).transpose(2, 1, 0, 3).reshape(4, 128, ND * 512))
    wk_r = f(Wk.reshape(16, 128, 512).transpose(1, 0, 2).reshape(128, ND * 512))
    wv_r = f(Wv.reshape(16, 128, 512).transpose(1, 0, 2).reshape(128, ND * 512))
    wo_r = f(Wo.reshape(2, 8, 128, 2048).transpose(0, 2, 1, 3).reshape(2, 128, 8 * 2048))
    wr_r = f(Wr.reshape(16, 128, 8).transpose(1, 0, 2).reshape(128, ND * 8))
    wsg_r = f(Wsg.reshape(16, 128, 1).transpose(1, 0, 2).reshape(128, ND))

    wg_e = Wg.reshape(8, 16, 128, 11, 128).transpose(0, 3, 2, 1, 4).reshape(8, 11, 128, 2048)
    sg_s = Sg.reshape(16, 128, 4, 11, 128).transpose(2, 3, 1, 0, 4).reshape(4, 11, 128, 2048)
    wg_r = np.concatenate([wg_e, sg_s], axis=0)
    wu_e = Wu.reshape(8, 16, 128, 11, 128).transpose(0, 3, 2, 1, 4).reshape(8, 11, 128, 2048)
    su_s = Su.reshape(16, 128, 4, 11, 128).transpose(2, 3, 1, 0, 4).reshape(4, 11, 128, 2048)
    wu_r = np.concatenate([wu_e, su_s], axis=0)
    wgu_r = f(np.concatenate([wg_r, wu_r], axis=3))
    wd_e = Wd.reshape(8, 11, 128, 2048).transpose(0, 2, 1, 3).reshape(8, 128, 11 * 2048)
    sd_s = Sd.reshape(4, 11, 128, 2048).transpose(0, 2, 1, 3).reshape(4, 128, 11 * 2048)
    wd_r = f(np.concatenate([wd_e, sd_s], axis=0))

    inv = 1.0 / (10000.0 ** (np.arange(0, 128, 2, dtype=np.float64) / 128.0))
    pos = np.arange(S, dtype=np.float64)
    fr = pos[:, None] * inv[None, :]
    emb = np.concatenate([fr, fr], axis=1)
    costab = np.cos(emb).astype(np.float32)
    sintab = np.sin(emb).astype(np.float32)

    identb = np.eye(128, dtype=ml_dtypes.bfloat16)
    identf = np.eye(128, dtype=np.float32)
    onesrow = np.ones((1, 128), dtype=ml_dtypes.bfloat16)
    onescol = np.ones((128, 1), dtype=ml_dtypes.bfloat16)
    sel3 = np.zeros((8, 8, 128), dtype=ml_dtypes.bfloat16)
    for e in range(8):
        sel3[e, e, :] = 1
    selm = np.ascontiguousarray(sel3.transpose(1, 0, 2).reshape(8, 8 * 128))
    consts = np.zeros((128, 2), dtype=np.float32)
    consts[:, 0] = 1e-5

    base = dict(wq_r=wq_r, wk_r=wk_r, wv_r=wv_r, wo_r=wo_r, wr_r=wr_r,
                wsg_r=wsg_r, wgu_r=wgu_r, wd_r=wd_r,
                identb=identb, identf=identf, onesrow=onesrow,
                onescol=onescol, sel=selm, consts=consts)
    return base, costab, sintab


def kernel(**inputs):
    x = np.asarray(inputs["L0_attn_x"], np.float32)
    base, costab, sintab = _prep_weights(inputs)
    if _NC_CACHE[0] is None:
        _NC_CACHE[0] = build_program()
    nc = _NC_CACHE[0]

    in_maps = []
    for c in range(8):
        b, q0 = c // 4, (c % 4) * 1024
        xs = np.roll(x[b], -q0, axis=0) if q0 else np.ascontiguousarray(x[b])
        ct = np.roll(costab, -q0, axis=0) if q0 else costab
        st = np.roll(sintab, -q0, axis=0) if q0 else sintab
        cos_rr = np.ascontiguousarray(ct.reshape(32, 128, 128).transpose(1, 0, 2).reshape(128, 4096))
        sin_rr = np.ascontiguousarray(st.reshape(32, 128, 128).transpose(1, 0, 2).reshape(128, 4096))
        m = dict(base)
        m["xseq"] = np.ascontiguousarray(xs)
        m["cos_r"] = cos_rr
        m["sin_r"] = sin_rr
        in_maps.append(m)

    res = run_bass_kernel_spmd(nc, in_maps, core_ids=list(range(8)))
    outp = np.empty((B, L, D), np.float32)
    for c in range(8):
        b, q0 = c // 4, (c % 4) * 1024
        outp[b, q0:q0 + 1024, :] = res.results[c]["out"]
    return outp


# version 16
# speedup vs baseline: 1.0145x; 1.0145x over previous
import contextlib
import numpy as np
import ml_dtypes
import concourse.bass as bass
import concourse.mybir as mybir
from concourse.bass_utils import run_bass_kernel_spmd

F32 = mybir.dt.float32
BF16 = mybir.dt.bfloat16
AF = mybir.ActivationFunctionType
ALU = mybir.AluOpType
AX = mybir.AxisListType

B, L, D = 2, 4096, 2048
H, HKV, HD = 16, 4, 128
T, S = 1024, 4096
NQ, NKS, ND = 8, 32, 16
FE, NF = 1408, 11
NEXP = 12
ISCALE = float(1.0 / np.sqrt(128.0))

ARENA_N = 98304
R_V = 0
R_KT = 16384
R_XQ = 32768
R_OO = 49152
R_WB = 65536
R_H = 32768
R_GSB = 44032
R_CB = 45056
R_LGF = 34816
R_CSB = 38912
R_GSROW = 43008
R_WK = 65536
R_WV = 73728
R_XS = 81920
R_STG = 90112
R_XN = 94208
R_WOA = 65536
R_WOB = 81920
R_PT = 81920
R_R1 = 83968
R_RECB = 84992
R_RBSB = 85504
R_RECB2 = 86016
R_RBSB2 = 86528
R_R12 = 87040
R_WD = 65536
R_WGU = 88064
R_GATE = 96256


def build_program():
    nc = bass.Bass("TRN2", target_bir_lowering=False)
    dt = nc.dram_tensor
    xseq = dt("xseq", [S, D], F32, kind="ExternalInput").ap()
    wq_r = dt("wq_r", [4, 128, ND * 512], F32, kind="ExternalInput").ap()
    wk_r = dt("wk_r", [128, ND * 512], F32, kind="ExternalInput").ap()
    wv_r = dt("wv_r", [128, ND * 512], F32, kind="ExternalInput").ap()
    wo_r = dt("wo_r", [2, 128, 8 * 2048], F32, kind="ExternalInput").ap()
    wr_rr = dt("wr_r", [128, ND * 8], F32, kind="ExternalInput").ap()
    wsg_rr = dt("wsg_r", [128, ND * 1], F32, kind="ExternalInput").ap()
    wgu_r = dt("wgu_r", [NEXP, NF, 128, 2 * ND * 128], F32, kind="ExternalInput").ap()
    wd_r = dt("wd_r", [NEXP, 128, NF * 2048], F32, kind="ExternalInput").ap()
    cos_r = dt("cos_r", [128, NKS * 128], F32, kind="ExternalInput").ap()
    sin_r = dt("sin_r", [128, NKS * 128], F32, kind="ExternalInput").ap()
    identb = dt("identb", [128, 128], BF16, kind="ExternalInput").ap()
    identf = dt("identf", [128, 128], F32, kind="ExternalInput").ap()
    onesrow = dt("onesrow", [1, 128], BF16, kind="ExternalInput").ap()
    onescol = dt("onescol", [128, 1], BF16, kind="ExternalInput").ap()
    sel = dt("sel", [8, 8 * 128], BF16, kind="ExternalInput").ap()
    consts = dt("consts", [128, 2], F32, kind="ExternalInput").ap()
    out = dt("out", [T, D], F32, kind="ExternalOutput").ap()

    with contextlib.ExitStack() as stack:
        E = stack.enter_context
        ar = E(nc.sbuf_tensor("arena", [128, ARENA_N], BF16)).ap()
        stat = E(nc.sbuf_tensor("stat", [128, 64], F32)).ap()
        cst = E(nc.sbuf_tensor("cst", [128, 2], F32)).ap()
        idb = E(nc.sbuf_tensor("idb", [128, 128], BF16)).ap()
        idf = E(nc.sbuf_tensor("idf", [128, 128], F32)).ap()
        ones_r = E(nc.sbuf_tensor("ones_r", [1, 128], BF16)).ap()
        ones_c = E(nc.sbuf_tensor("ones_c", [128, 1], BF16)).ap()
        sel_sb = E(nc.sbuf_tensor("sel_sb", [8, 8 * 128], BF16)).ap()
        tcb = E(nc.sbuf_tensor("tcb", [128, 512], F32)).ap()
        tsb = E(nc.sbuf_tensor("tsb", [128, 512], F32)).ap()
        dacc = E(nc.sbuf_tensor("dacc", [128, 1024], F32)).ap()
        krotb = E(nc.sbuf_tensor("krotb", [128, 512], BF16)).ap()
        krotb2 = E(nc.sbuf_tensor("krotb2", [128, 512], BF16)).ap()
        lgtok = E(nc.sbuf_tensor("lgtok", [128, 64], F32)).ap()
        rtmp = E(nc.sbuf_tensor("rtmp", [128, 64], F32)).ap()
        rst = E(nc.sbuf_tensor("rst", [128, 64], F32)).ap()
        ctok = E(nc.sbuf_tensor("ctok", [128, 64], F32)).ap()
        cbf = tcb[0:8, :].bitcast(BF16)
        wrsb = E(nc.sbuf_tensor("wrsb", [128, ND * 8], BF16)).ap()
        wsgsb = E(nc.sbuf_tensor("wsgsb", [128, ND], BF16)).ap()
        ps = E(nc.psum_tensor("ps", [128, 4096], F32)).ap()
        sLD = E(nc.semaphore("sLD"))
        sWT = E(nc.semaphore("sWT"))
        sTT = E(nc.semaphore("sTT"))
        sVV = E(nc.semaphore("sVV"))
        sSS = E(nc.semaphore("sSS"))
        sST = E(nc.semaphore("sST"))
        sW6a = E(nc.semaphore("sW6a"))
        sW6b = E(nc.semaphore("sW6b"))
        sW6d = E(nc.semaphore("sW6d"))

        def bf(off, n):
            return ar[:, off:off + n]

        def f32v(off, nf32):
            return ar[:, off:off + 2 * nf32].bitcast(F32)

        def PB(b, w=1):
            return ps[:, 512 * b:512 * (b + w)]

        def PBH(b, w=1):
            return ps[:, 512 * b:512 * (b + w)].bitcast(BF16)

        nc.const_aps.register(F32, 1e-5, cst[:, 0:1])
        cosv = f32v(R_OO, 4096)
        sinv = f32v(R_OO + 8192, 4096)
        xsv = f32v(R_XS, 4096)
        x1v = f32v(R_V, 16384)

        nLD = nWT = nTT = nVV = nSS = 0
        wt_ev = {}

        bL, bT, bV, bS = nLD, nTT, nVV, nSS
        ev_x = lambda ks: bL + 16 * (ks + 1)

        t1_list = []
        for g in range(NKS + 2):
            if g < NKS:
                t1_list += [("trA", g), ("trB", g)]
            if 0 <= g - 1 < NKS:
                t1_list.append(("kv", g - 1))
            if 0 <= g - 2 < NKS:
                t1_list.append(("ktr", g - 2))
        evT1 = {k: bT + i + 1 for i, k in enumerate(t1_list)}
        v1_list = []
        for g in range(NKS + 3):
            if 0 <= g - 1 < NKS:
                v1_list += [("evA", g - 1), ("evB", g - 1)]
            if g < NKS:
                v1_list += [("bn", g), ("xn", g)]
            if 0 <= g - 2 < NKS:
                v1_list += [("vev", g - 2), ("krope", g - 2)]
            if 0 <= g - 3 < NKS:
                v1_list.append(("ktev", g - 3))
        evV1 = {k: bV + i + 1 for i, k in enumerate(v1_list)}
        ev_xn = lambda ks: evV1[("xn", ks)]
        ev_bn = lambda ks: evV1[("bn", ks)]
        ev_trB = lambda ks: evT1[("trB", ks)]
        ev_rstd = lambda ks: bS + ks + 1

        with nc.Block("p1a") as blk:
            @blk.sync
            def _(sy):
                nn = bL
                for ks in range(NKS):
                    if ks >= 2:
                        sy.wait_ge(sVV, ev_xn(ks - 2))
                    sy.wait_ge(sLD, nn)
                    sy.dma_start(out=xsv[:, (ks % 2) * 2048:(ks % 2) * 2048 + 2048],
                                 in_=xseq[ks * 128:(ks + 1) * 128, :]).then_inc(sLD, 16)
                    nn += 16

            @blk.gpsimd
            def _(g):
                pre = [(idb, identb[:, :]), (cst, consts[:, :]),
                       (cosv, cos_r[:, :]), (sinv, sin_r[:, :]),
                       (idf, identf[:, :]), (ones_r, onesrow[:, :]),
                       (ones_c, onescol[:, :]), (sel_sb, sel[:, :]),
                       (bf(R_WK, 8192), wk_r[:, :]), (bf(R_WV, 8192), wv_r[:, :])]
                nn = nWT
                for o_, i_ in pre:
                    g.wait_ge(sWT, nn)
                    g.dma_start(out=o_, in_=i_).then_inc(sWT, 16)
                    nn += 16
                g.wait_ge(sTT, evT1[("kv", NKS - 1)])
                g.wait_ge(sVV, evV1[("xn", NKS - 1)])
                for cb in range(4):
                    g.wait_ge(sWT, nn)
                    g.dma_start(out=bf(R_WB + cb * 8192, 8192),
                                in_=wq_r[cb, :, :]).then_inc(sWT, 16)
                    nn += 16

            nWT += 14 * 16
            wt_ev["idb"] = nWT - 224 + 16
            wt_ev["consts"] = nWT - 224 + 64
            wt_ev["wkv"] = nWT - 64
            wt_ev["wq"] = nWT

            @blk.tensor
            def _(t):
                t.wait_ge(sWT, wt_ev["idb"])
                first_kv = [True]
                for item in t1_list:
                    kind, g = item
                    if kind == "kv" and first_kv[0]:
                        t.wait_ge(sWT, wt_ev["wkv"])
                        first_kv[0] = False
                    if kind == "trA":
                        xn = bf(R_XN + (g % 2) * 2048, 2048)
                        t.wait_ge(sVV, ev_xn(g))
                        if g >= 1:
                            t.wait_ge(sVV, evV1[("evB", g - 1)])
                        for dc in range(8):
                            ins = t.transpose(PBH(0)[:, dc * 128:(dc + 1) * 128],
                                              xn[:, dc * 128:(dc + 1) * 128], idb)
                        ins.then_inc(sTT, 1)
                    elif kind == "trB":
                        xn = bf(R_XN + (g % 2) * 2048, 2048)
                        for dc in range(8, 16):
                            ins = t.transpose(PBH(1)[:, (dc - 8) * 128:(dc - 7) * 128],
                                              xn[:, dc * 128:(dc + 1) * 128], idb)
                        ins.then_inc(sTT, 1)
                    elif kind == "kv":
                        ks = g
                        own = ks < NQ
                        t.wait_ge(sVV, evV1[("evB", ks)])
                        if ks >= 2:
                            t.wait_ge(sVV, evV1[("krope", ks - 2)])
                            t.wait_ge(sVV, evV1[("vev", ks - 2)])
                        kb = 2 + (ks % 2) * 2
                        vb = 3 + (ks % 2) * 2
                        for dc in range(ND):
                            if own:
                                lhs = bf(R_XQ + dc * 1024 + ks * 128, 128)
                            else:
                                lhs = bf(R_STG + (ks % 2) * 2048 + dc * 128, 128)
                            t.matmul(PB(kb), lhs, bf(R_WK + dc * 512, 512),
                                     start=(dc == 0), stop=(dc == ND - 1))
                            ins = t.matmul(PB(vb), lhs, bf(R_WV + dc * 512, 512),
                                           start=(dc == 0), stop=(dc == ND - 1))
                        ins.then_inc(sTT, 1)
                    else:
                        ks = g
                        t.wait_ge(sVV, evV1[("krope", ks)])
                        if ks >= 2:
                            t.wait_ge(sVV, evV1[("ktev", ks - 2)])
                        ktb = 6 + (ks % 2)
                        kr = krotb if ks % 2 == 0 else krotb2
                        for j in range(4):
                            ins = t.transpose(PBH(ktb)[:, j * 128:(j + 1) * 128],
                                              kr[:, j * 128:(j + 1) * 128], idb)
                        ins.then_inc(sTT, 1)

            @blk.vector
            def _(v):
                v.wait_ge(sWT, wt_ev["consts"])
                for item in v1_list:
                    kind, g = item
                    if kind == "evA":
                        ks = g
                        v.wait_ge(sTT, evT1[("trA", ks)])
                        if ks < NQ:
                            dst = bf(R_XQ, 16384).rearrange("p (c q) -> p c q", c=16)[:, 0:8, ks * 128:(ks + 1) * 128]
                        else:
                            dst = bf(R_STG + (ks % 2) * 2048, 1024).rearrange("p (c q) -> p c q", c=8)
                        v.tensor_copy(dst, PBH(0).rearrange("p (c q) -> p c q", c=8)).then_inc(sVV, 1)
                    elif kind == "evB":
                        ks = g
                        v.wait_ge(sTT, evT1[("trB", ks)])
                        if ks < NQ:
                            dst = bf(R_XQ, 16384).rearrange("p (c q) -> p c q", c=16)[:, 8:16, ks * 128:(ks + 1) * 128]
                        else:
                            dst = bf(R_STG + (ks % 2) * 2048 + 1024, 1024).rearrange("p (c q) -> p c q", c=8)
                        v.tensor_copy(dst, PBH(1).rearrange("p (c q) -> p c q", c=8)).then_inc(sVV, 1)
                    elif kind == "bn":
                        ks = g
                        xt = xsv[:, (ks % 2) * 2048:(ks % 2) * 2048 + 2048]
                        sb = (ks % 2) * 32
                        v.wait_ge(sLD, ev_x(ks))
                        v.reduce_sum(stat[:, sb + 29:sb + 30], xt, axis=AX.X)
                        v.drain()
                        v.tensor_scalar(stat[:, sb + 24:sb + 25], stat[:, sb + 29:sb + 30],
                                        1.0 / 2048.0, None, ALU.mult).then_inc(sVV, 1)
                    elif kind == "xn":
                        ks = g
                        xt = xsv[:, (ks % 2) * 2048:(ks % 2) * 2048 + 2048]
                        xn = bf(R_XN + (ks % 2) * 2048, 2048)
                        sb = (ks % 2) * 32
                        v.wait_ge(sSS, ev_rstd(ks))
                        v.tensor_scalar(xn, xt, stat[:, sb + 24:sb + 25],
                                        stat[:, sb + 26:sb + 27],
                                        ALU.subtract, ALU.mult).then_inc(sVV, 1)
                    elif kind == "vev":
                        ks = g
                        v.wait_ge(sTT, evT1[("kv", ks)])
                        v.tensor_copy(bf(R_V + ks * 512, 512), PB(3 + (ks % 2) * 2)).then_inc(sVV, 1)
                    elif kind == "krope":
                        ks = g
                        kb = 2 + (ks % 2) * 2
                        if ks >= 2:
                            v.wait_ge(sTT, evT1[("ktr", ks - 2)])
                        cs = cosv[:, ks * 128:(ks + 1) * 128].rearrange("p (r d) -> p r d", r=1).broadcast_to([128, 4, 128])
                        sn = sinv[:, ks * 128:(ks + 1) * 128]
                        v.tensor_tensor(tcb.rearrange("p (r d) -> p r d", r=4),
                                        PB(kb).rearrange("p (r d) -> p r d", r=4), cs, ALU.mult)
                        for j in range(4):
                            v.tensor_tensor(tsb[:, j * 128 + 1:(j + 1) * 128],
                                            PB(kb)[:, j * 128:j * 128 + 127],
                                            sn[:, 1:128], ALU.mult)
                            v.tensor_tensor(tsb[:, j * 128:j * 128 + 1],
                                            PB(kb)[:, j * 128 + 127:(j + 1) * 128],
                                            sn[:, 0:1], ALU.mult)
                        v.drain()
                        kr = krotb if ks % 2 == 0 else krotb2
                        v.tensor_tensor(kr, tcb, tsb, ALU.add).then_inc(sVV, 1)
                    else:
                        ks = g
                        v.wait_ge(sTT, evT1[("ktr", ks)])
                        ktb = 6 + (ks % 2)
                        dst = bf(R_KT, 16384).rearrange("p (j k) -> p j k", j=4)[:, :, ks * 128:(ks + 1) * 128]
                        v.tensor_copy(dst, PBH(ktb)[:, 0:512].rearrange("p (j k) -> p j k", j=4)).then_inc(sVV, 1)

            @blk.scalar
            def _(s):
                s.wait_ge(sWT, wt_ev["consts"])
                for ks in range(NKS):
                    sb = (ks % 2) * 32
                    xt = xsv[:, (ks % 2) * 2048:(ks % 2) * 2048 + 2048]
                    xn = bf(R_XN + (ks % 2) * 2048, 2048)
                    s.wait_ge(sLD, ev_x(ks))
                    if ks >= 2:
                        s.wait_ge(sTT, ev_trB(ks - 2))
                    s.activation(xn, xt, AF.Square,
                                 accum_out=stat[:, sb + 28:sb + 29])
                    s.drain()
                    s.wait_ge(sVV, ev_bn(ks))
                    s.mul(stat[:, sb + 31:sb + 32], stat[:, sb + 24:sb + 25], -1.0)
                    s.mul(stat[:, sb + 27:sb + 28], stat[:, sb + 24:sb + 25],
                          stat[:, sb + 31:sb + 32])
                    s.activation(stat[:, sb + 30:sb + 31], stat[:, sb + 27:sb + 28],
                                 AF.Copy, bias=1e-5)
                    s.drain()
                    s.activation(stat[:, sb + 27:sb + 28], stat[:, sb + 28:sb + 29],
                                 AF.Ln, bias=stat[:, sb + 30:sb + 31], scale=1.0 / 2048.0)
                    s.activation(stat[:, sb + 26:sb + 27], stat[:, sb + 27:sb + 28],
                                 AF.Exp, scale=-0.5)
                    s.drain().then_inc(sSS, 1)

        nLD = bL + 16 * NKS
        nTT = bT + 4 * NKS
        nVV = bV + 7 * NKS
        nSS = bS + NKS

        bT, bV = nTT, nVV
        ev_q = lambda qs: bT + 5 * qs + 1
        ev_qtr = lambda qs, cb: bT + 5 * qs + 2 + cb
        ev_qrope = lambda qs, cb: bV + 8 * qs + 2 * cb + 1
        ev_qtev = lambda qs, cb: bV + 8 * qs + 2 * cb + 2

        with nc.Block("p1b") as blk:
            @blk.tensor
            def _(t):
                t.wait_ge(sWT, wt_ev["wq"])
                for qs in range(NQ):
                    for dc in range(ND):
                        lhs = bf(R_XQ + dc * 1024 + qs * 128, 128)
                        for cb in range(4):
                            ins = t.matmul(PB(cb), lhs, bf(R_WB + cb * 8192 + dc * 512, 512),
                                           start=(dc == 0), stop=(dc == ND - 1))
                    ins.then_inc(sTT, 1)
                    for cb in range(4):
                        t.wait_ge(sVV, ev_qrope(qs, cb))
                        ftr = 4 * qs + cb
                        if ftr >= 2:
                            qsp, cbp = divmod(ftr - 2, 4)
                            t.wait_ge(sVV, ev_qtev(qsp, cbp))
                        ktb = 6 + (ftr % 2)
                        for j in range(4):
                            ins = t.transpose(PBH(ktb)[:, j * 128:(j + 1) * 128],
                                              krotb[:, j * 128:(j + 1) * 128], idb)
                        ins.then_inc(sTT, 1)

            @blk.vector
            def _(v):
                for qs in range(NQ):
                    v.wait_ge(sTT, ev_q(qs))
                    cs = cosv[:, qs * 128:(qs + 1) * 128].rearrange("p (r d) -> p r d", r=1).broadcast_to([128, 4, 128])
                    sn = sinv[:, qs * 128:(qs + 1) * 128]
                    for cb in range(4):
                        qp = PB(cb)
                        v.tensor_tensor(tcb.rearrange("p (r d) -> p r d", r=4),
                                        qp.rearrange("p (r d) -> p r d", r=4), cs, ALU.mult)
                        for j in range(4):
                            v.tensor_tensor(tsb[:, j * 128 + 1:(j + 1) * 128],
                                            qp[:, j * 128:j * 128 + 127], sn[:, 1:128], ALU.mult)
                            v.tensor_tensor(tsb[:, j * 128:j * 128 + 1],
                                            qp[:, j * 128 + 127:(j + 1) * 128], sn[:, 0:1], ALU.mult)
                        v.drain()
                        v.tensor_tensor(krotb, tcb, tsb, ALU.add).then_inc(sVV, 1)
                        v.wait_ge(sTT, ev_qtr(qs, cb))
                        ktb = 6 + ((4 * qs + cb) % 2)
                        dst = bf(R_XQ, 16384).rearrange("p (hh q) -> p hh q", hh=16)[:, 4 * cb:4 * cb + 4, qs * 128:(qs + 1) * 128]
                        v.tensor_copy(dst, PBH(ktb)[:, 0:512].rearrange("p (j k) -> p j k", j=4)).then_inc(sVV, 1)

        nTT = bT + 5 * NQ
        nVV = bV + 8 * NQ

        bT, bV, bS = nTT, nVV, nSS
        t2_list = []
        for h in range(H):
            t2_list += [("s", h, 0), ("s", h, 1)]
            for kt in range(NKS):
                t2_list.append(("g", h, kt))
                if kt < 30:
                    t2_list.append(("s", h, kt + 2))
            t2_list += [("d", h), ("rb", h)]
        evT2 = {k: bT + i + 1 for i, k in enumerate(t2_list)}
        ev2_exp = lambda h, kt: bS + 33 * h + kt + 1
        ev2_recb = lambda h: bS + 33 * h + 33
        ev2_add = lambda h, kt: bV + 36 * h + kt + 1
        ev2_comb = lambda h: bV + 36 * h + 33
        ev2_rbe = lambda h: bV + 36 * h + 34
        ev2_on0 = lambda h: bV + 36 * h + 35
        ev2_on1 = lambda h: bV + 36 * h + 36

        def pt_buf2(n):
            return bf(R_WGU + (n % 4) * 1024, 1024)

        r1v = f32v(R_R1, 512)[0:1, :]
        r1v2 = f32v(R_R12, 512)[0:1, :]
        recb0 = bf(R_RECB, 512)[0:1, :]
        recb1 = bf(R_RECB2, 512)[0:1, :]
        rbsb0 = bf(R_RBSB, 512)
        rbsb1 = bf(R_RBSB2, 512)

        with nc.Block("p2") as blk:
            @blk.gpsimd
            def _(g):
                g.wait_ge(sWT, nWT)
                g.dma_start(out=bf(R_WOA, 16384), in_=wo_r[0, :, :]).then_inc(sWT, 16)

            nWT += 16
            wt_ev["woA"] = nWT

            @blk.tensor
            def _(t):
                for item in t2_list:
                    if item[0] == "s":
                        _, h, kt = item
                        j = h // 4
                        if kt >= 2:
                            t.wait_ge(sSS, ev2_exp(h, kt - 2))
                        elif h >= 1:
                            t.wait_ge(sSS, ev2_exp(h - 1, 30 + kt))
                        for qt in range(2):
                            ins = t.matmul(PB(2 * (kt % 2) + qt),
                                           bf(R_KT + j * 4096 + kt * 128, 128),
                                           bf(R_XQ + h * 1024 + qt * 512, 512),
                                           start=True, stop=True)
                        ins.then_inc(sTT, 1)
                    elif item[0] == "g":
                        _, h, kt = item
                        j = h // 4
                        n = h * 32 + kt
                        t.wait_ge(sSS, ev2_exp(h, kt))
                        if kt == 0 and h >= 1:
                            t.wait_ge(sVV, ev2_on1(h - 1))
                        for qt in range(2):
                            ins = t.matmul(PB(4 + qt), bf(R_V + kt * 512 + j * 128, 128),
                                           pt_buf2(n)[:, qt * 512:(qt + 1) * 512],
                                           start=(kt == 0), stop=(kt == 31))
                        ins.then_inc(sTT, 1)
                    elif item[0] == "d":
                        h = item[1]
                        t.wait_ge(sVV, ev2_comb(h))
                        if h >= 1:
                            t.wait_ge(sVV, ev2_rbe(h - 1))
                        combv = bf(R_PT, 1024)
                        for qt in range(2):
                            ins = t.matmul(ps[0:1, 512 * (6 + qt):512 * (6 + qt) + 512],
                                           ones_c, combv[:, qt * 512:(qt + 1) * 512],
                                           start=True, stop=True)
                        ins.then_inc(sTT, 1)
                    else:
                        h = item[1]
                        t.wait_ge(sSS, ev2_recb(h))
                        t.matmul(PB(6), ones_r, recb0, start=True, stop=True)
                        t.matmul(PB(7), ones_r, recb1, start=True, stop=True).then_inc(sTT, 1)

            @blk.scalar
            def _(s):
                for h in range(H):
                    for kt in range(NKS):
                        n = h * 32 + kt
                        s.wait_ge(sTT, evT2[("s", h, kt)])
                        if n >= 4:
                            hp, ktp = divmod(n - 4, 32)
                            s.wait_ge(sTT, evT2[("g", hp, ktp)])
                            s.wait_ge(sVV, ev2_add(hp, ktp))
                        bk0 = 2 * (kt % 2)
                        s.activation(pt_buf2(n), ps[:, 512 * bk0:512 * (bk0 + 2)],
                                     AF.Exp, scale=ISCALE).then_inc(sSS, 1)
                    s.wait_ge(sTT, evT2[("d", h)])
                    s.activation(r1v, ps[0:1, 512 * 6:512 * 6 + 512], AF.Ln)
                    s.activation(recb0, r1v, AF.Exp, scale=-1.0)
                    s.activation(r1v2, ps[0:1, 512 * 7:512 * 7 + 512], AF.Ln)
                    s.activation(recb1, r1v2, AF.Exp, scale=-1.0)
                    s.drain().then_inc(sSS, 1)

            @blk.vector
            def _(v):
                for h in range(H):
                    for kt in range(NKS):
                        n = h * 32 + kt
                        v.wait_ge(sSS, ev2_exp(h, kt))
                        if kt == 0:
                            v.tensor_copy(dacc[:, :], pt_buf2(n)).then_inc(sVV, 1)
                        else:
                            v.tensor_tensor(dacc[:, :], dacc[:, :], pt_buf2(n),
                                            ALU.add).then_inc(sVV, 1)
                    v.tensor_copy(bf(R_PT, 1024), dacc[:, :]).then_inc(sVV, 1)
                    v.wait_ge(sTT, evT2[("rb", h)])
                    v.tensor_copy(rbsb0, PB(6))
                    v.drain()
                    v.tensor_copy(rbsb1, PB(7))
                    v.drain().then_inc(sVV, 1)
                    v.tensor_tensor(bf(R_OO + h * 1024, 512),
                                    PB(4), rbsb0, ALU.mult).then_inc(sVV, 1)
                    v.tensor_tensor(bf(R_OO + h * 1024 + 512, 512),
                                    PB(5), rbsb1, ALU.mult).then_inc(sVV, 1)

        nTT = bT + len(t2_list)
        nSS = bS + 33 * H
        nVV = bV + 36 * H

        bL, bT, bV = nLD, nTT, nVV
        ev_p3 = lambda p, qs: bT + p * NQ + qs + 1
        ev_p3add = lambda p, qs: bV + p * NQ + qs + 1
        ev_xown = lambda qs: bL + 16 * (qs + 1)

        with nc.Block("p3") as blk:
            @blk.sync
            def _(sy):
                for qs in range(NQ):
                    sy.wait_ge(sLD, bL + 16 * qs)
                    sy.dma_start(out=x1v[:, qs * 2048:(qs + 1) * 2048],
                                 in_=xseq[qs * 128:(qs + 1) * 128, :]).then_inc(sLD, 16)

            @blk.gpsimd
            def _(g):
                g.wait_ge(sWT, wt_ev["woA"])
                g.dma_start(out=bf(R_WOB, 16384), in_=wo_r[1, :, :]).then_inc(sWT, 16)

            nWT += 16
            wt_ev["woB"] = nWT

            @blk.tensor
            def _(t):
                t.wait_ge(sWT, wt_ev["woA"])
                for p in range(2):
                    if p == 1:
                        t.wait_ge(sWT, wt_ev["woB"])
                    for qs in range(NQ):
                        bk = (qs % 2) * 4
                        gidx = p * NQ + qs
                        if gidx >= 2:
                            pp, qq = divmod(gidx - 2, NQ)
                            t.wait_ge(sVV, ev_p3add(pp, qq))
                        for hh in range(8):
                            lhs = bf(R_OO + (p * 8 + hh) * 1024 + qs * 128, 128)
                            for doc in range(4):
                                ins = t.matmul(PB(bk + doc), lhs,
                                               bf((R_WOA if p == 0 else R_WOB) + hh * 2048 + doc * 512, 512),
                                               start=(hh == 0), stop=(hh == 7))
                        ins.then_inc(sTT, 1)

            @blk.vector
            def _(v):
                for p in range(2):
                    for qs in range(NQ):
                        bk = (qs % 2) * 4
                        v.wait_ge(sTT, ev_p3(p, qs))
                        if p == 0:
                            v.wait_ge(sLD, ev_xown(qs))
                        x1s = x1v[:, qs * 2048:(qs + 1) * 2048]
                        v.tensor_tensor(x1s, x1s, PB(bk, 4), ALU.add).then_inc(sVV, 1)

        nLD = bL + 16 * NQ
        nTT = bT + 2 * NQ
        nVV = bV + 2 * NQ

        bT, bV, bS = nTT, nVV, nSS
        ev4_bn = lambda qs: bV + 4 * qs + 1
        ev4_xn = lambda qs: bV + 4 * qs + 2
        ev4_evA = lambda qs: bV + 4 * qs + 3
        ev4_evB = lambda qs: bV + 4 * qs + 4
        ev4_rstd = lambda qs: bS + qs + 1
        ev4_trA = lambda qs: bT + 2 * qs + 1
        ev4_trB = lambda qs: bT + 2 * qs + 2

        def xnb(qs):
            return bf(R_WGU + (qs % 2) * 2048, 2048)

        with nc.Block("p4") as blk:
            @blk.gpsimd
            def _(g):
                g.dma_start(out=bf(R_WD, NF * 2048), in_=wd_r[0, :, :]).then_inc(sW6d, 16)

            wt_ev["wd0"] = 16

            @blk.tensor
            def _(t):
                for qs in range(NQ):
                    t.wait_ge(sVV, ev4_xn(qs))
                    if qs >= 1:
                        t.wait_ge(sVV, ev4_evA(qs - 1))
                    for dc in range(8):
                        ins = t.transpose(PBH(0)[:, dc * 128:(dc + 1) * 128],
                                          xnb(qs)[:, dc * 128:(dc + 1) * 128], idb)
                    ins.then_inc(sTT, 1)
                    if qs >= 1:
                        t.wait_ge(sVV, ev4_evB(qs - 1))
                    for dc in range(8, 16):
                        ins = t.transpose(PBH(1)[:, (dc - 8) * 128:(dc - 7) * 128],
                                          xnb(qs)[:, dc * 128:(dc + 1) * 128], idb)
                    ins.then_inc(sTT, 1)

            @blk.vector
            def _(v):
                for qs in range(NQ):
                    x1s = x1v[:, qs * 2048:(qs + 1) * 2048]
                    sb = (qs % 2) * 32
                    v.reduce_sum(stat[:, sb + 29:sb + 30], x1s, axis=AX.X)
                    v.drain()
                    v.tensor_scalar(stat[:, sb + 24:sb + 25], stat[:, sb + 29:sb + 30],
                                    1.0 / 2048.0, None, ALU.mult).then_inc(sVV, 1)
                    v.wait_ge(sSS, ev4_rstd(qs))
                    v.tensor_scalar(xnb(qs), x1s, stat[:, sb + 24:sb + 25],
                                    stat[:, sb + 26:sb + 27],
                                    ALU.subtract, ALU.mult).then_inc(sVV, 1)
                    v.wait_ge(sTT, ev4_trA(qs))
                    dst = bf(R_OO, 16384).rearrange("p (c q) -> p c q", c=16)[:, 0:8, qs * 128:(qs + 1) * 128]
                    v.tensor_copy(dst, PBH(0).rearrange("p (c q) -> p c q", c=8)).then_inc(sVV, 1)
                    v.wait_ge(sTT, ev4_trB(qs))
                    dst = bf(R_OO, 16384).rearrange("p (c q) -> p c q", c=16)[:, 8:16, qs * 128:(qs + 1) * 128]
                    v.tensor_copy(dst, PBH(1).rearrange("p (c q) -> p c q", c=8)).then_inc(sVV, 1)

            @blk.scalar
            def _(s):
                for qs in range(NQ):
                    sb = (qs % 2) * 32
                    x1s = x1v[:, qs * 2048:(qs + 1) * 2048]
                    if qs >= 2:
                        s.wait_ge(sTT, ev4_trB(qs - 2))
                    s.activation(xnb(qs), x1s, AF.Square,
                                 accum_out=stat[:, sb + 28:sb + 29])
                    s.drain()
                    s.wait_ge(sVV, ev4_bn(qs))
                    s.mul(stat[:, sb + 31:sb + 32], stat[:, sb + 24:sb + 25], -1.0)
                    s.mul(stat[:, sb + 27:sb + 28], stat[:, sb + 24:sb + 25],
                          stat[:, sb + 31:sb + 32])
                    s.activation(stat[:, sb + 30:sb + 31], stat[:, sb + 27:sb + 28],
                                 AF.Copy, bias=1e-5)
                    s.drain()
                    s.activation(stat[:, sb + 27:sb + 28], stat[:, sb + 28:sb + 29],
                                 AF.Ln, bias=stat[:, sb + 30:sb + 31], scale=1.0 / 2048.0)
                    s.activation(stat[:, sb + 26:sb + 27], stat[:, sb + 27:sb + 28],
                                 AF.Exp, scale=-0.5)
                    s.drain().then_inc(sSS, 1)

        nTT = bT + 2 * NQ
        nVV = bV + 4 * NQ
        nSS = bS + NQ

        bT, bV, bS = nTT, nVV, nSS
        lgF = ar[0:8, R_LGF:R_LGF + 2048].bitcast(F32)
        csbF = ar[0:8, R_CSB:R_CSB + 2048].bitcast(F32)
        gsrow = ar[0:1, 46080:46080 + 2048].bitcast(F32)
        gsrowb = ar[0:1, 48128:48128 + 1024]
        e5_lg = bT + 1
        e5_sig = bT + 2
        e5_lgt = bT + 3
        e5_ct = bT + 4
        e5_gsb = bT + 5
        e5_lge = bV + 1
        e5_neg = bV + 2
        e5_cu = bV + 3
        e5_ctok = bV + 4
        e5_csbe = bV + 5
        e5_cbf = bV + 6
        e5_gsrb = bV + 7
        e5_gsbe = bV + 8
        e5_exp = bS + 1
        e5_rcs = bS + 2
        e5_sg = bS + 3

        with nc.Block("p5") as blk:
            @blk.gpsimd
            def _(g):
                nn = nWT
                for o_, i_ in [(wrsb, wr_rr[:, :]), (wsgsb, wsg_rr[:, :])]:
                    g.wait_ge(sWT, nn)
                    g.dma_start(out=o_, in_=i_).then_inc(sWT, 16)
                    nn += 16
                g.dma_start(out=bf(R_WGU, 4096), in_=wgu_r[0, 0, :, :]).then_inc(sW6a, 16)
                g.dma_start(out=bf(R_WGU + 4096, 4096), in_=wgu_r[0, 1, :, :]).then_inc(sW6b, 16)

            nWT += 32
            wt_ev["wrsg"] = nWT
            wt_ev[("wgu", 0)] = 16
            wt_ev[("wgu", 1)] = 16

            @blk.tensor
            def _(t):
                t.wait_ge(sWT, wt_ev["wrsg"])
                for dc in range(ND):
                    for th in range(2):
                        ins = t.matmul(ps[0:8, th * 512:(th + 1) * 512],
                                       wrsb[:, dc * 8:(dc + 1) * 8],
                                       bf(R_OO + dc * 1024 + th * 512, 512),
                                       start=(dc == 0), stop=(dc == ND - 1))
                ins.then_inc(sTT, 1)
                for dc in range(ND):
                    for th in range(2):
                        ins = t.matmul(ps[0:1, 1024 + th * 512:1024 + (th + 1) * 512],
                                       wsgsb[:, dc:dc + 1],
                                       bf(R_OO + dc * 1024 + th * 512, 512),
                                       start=(dc == 0), stop=(dc == ND - 1))
                ins.then_inc(sTT, 1)
                t.wait_ge(sVV, e5_lge)
                for ts in range(8):
                    ins = t.transpose(ps[:, 2048 + ts * 8:2048 + ts * 8 + 8],
                                      lgF[:, ts * 128:(ts + 1) * 128], idf[0:8, 0:8])
                ins.then_inc(sTT, 1)
                t.wait_ge(sVV, e5_ctok)
                for ts in range(8):
                    ins = t.transpose(ps[0:8, 2560 + ts * 128:2560 + (ts + 1) * 128],
                                      ctok[:, ts * 8:(ts + 1) * 8], idf)
                ins.then_inc(sTT, 1)
                t.wait_ge(sVV, e5_gsrb)
                for th in range(2):
                    ins = t.matmul(PB(th), ones_r, gsrowb[:, th * 512:(th + 1) * 512],
                                   start=True, stop=True)
                ins.then_inc(sTT, 1)

            @blk.vector
            def _(v):
                v.wait_ge(sTT, e5_lg)
                v.tensor_copy(lgF, ps[0:8, 0:1024]).then_inc(sVV, 1)
                v.wait_ge(sTT, e5_lgt)
                v.tensor_copy(lgtok, ps[:, 2048:2112])
                v.drain()
                for ts in range(8):
                    c = lgtok[:, ts * 8:(ts + 1) * 8]
                    v.reduce_max(rst[:, ts * 8:ts * 8 + 1], c, axis=AX.X)
                v.drain()
                for ts in range(8):
                    ins = v.tensor_scalar(rst[:, ts * 8 + 1:ts * 8 + 2],
                                          rst[:, ts * 8:ts * 8 + 1], -1.0, None, ALU.mult)
                v.drain().then_inc(sVV, 1)
                v.wait_ge(sSS, e5_exp)
                for ts in range(8):
                    e = rtmp[:, ts * 8:(ts + 1) * 8]
                    v.reduce_sum(rst[:, ts * 8 + 2:ts * 8 + 3], e, axis=AX.X)
                    v.reduce_max(rst[:, ts * 8 + 3:ts * 8 + 4], e, axis=AX.X)
                    v.drain()
                    v.scalar_tensor_tensor(lgtok[:, ts * 8:(ts + 1) * 8], e,
                                           rst[:, ts * 8 + 3:ts * 8 + 4], e,
                                           ALU.is_lt, ALU.mult)
                    v.drain()
                    v.reduce_max(rst[:, ts * 8 + 4:ts * 8 + 5],
                                 lgtok[:, ts * 8:(ts + 1) * 8], axis=AX.X)
                    v.drain()
                    ins = v.scalar_tensor_tensor(ctok[:, ts * 8:(ts + 1) * 8], e,
                                                 rst[:, ts * 8 + 4:ts * 8 + 5], e,
                                                 ALU.is_ge, ALU.mult)
                v.drain().then_inc(sVV, 1)
                v.wait_ge(sSS, e5_rcs)
                for ts in range(8):
                    ins = v.tensor_scalar(ctok[:, ts * 8:(ts + 1) * 8],
                                          ctok[:, ts * 8:(ts + 1) * 8],
                                          rst[:, ts * 8 + 5:ts * 8 + 6], None, ALU.mult)
                ins.then_inc(sVV, 1)
                v.wait_ge(sTT, e5_ct)
                v.tensor_copy(csbF, ps[0:8, 2560:3584]).then_inc(sVV, 1)
                v.drain()
                v.tensor_copy(cbf, csbF)
                v.drain().then_inc(sVV, 1)
                v.wait_ge(sSS, e5_sg)
                v.tensor_copy(gsrowb, gsrow).then_inc(sVV, 1)
                v.wait_ge(sTT, e5_gsb)
                v.tensor_copy(bf(R_GSB, 1024), PBH(0, 2)[:, 0:1024] if False else PB(0, 2)).then_inc(sVV, 1)

            @blk.scalar
            def _(s):
                s.wait_ge(sVV, e5_neg)
                for ts in range(8):
                    ins = s.activation(rtmp[:, ts * 8:(ts + 1) * 8],
                                       lgtok[:, ts * 8:(ts + 1) * 8], AF.Exp,
                                       bias=rst[:, ts * 8 + 1:ts * 8 + 2])
                ins.then_inc(sSS, 1)
                s.wait_ge(sVV, e5_cu)
                for ts in range(8):
                    s.activation(rst[:, ts * 8 + 6:ts * 8 + 7],
                                 rst[:, ts * 8 + 2:ts * 8 + 3], AF.Ln)
                    ins = s.activation(rst[:, ts * 8 + 5:ts * 8 + 6],
                                       rst[:, ts * 8 + 6:ts * 8 + 7], AF.Exp, scale=-1.0)
                ins.then_inc(sSS, 1)
                s.wait_ge(sTT, e5_sig)
                s.activation(gsrow, ps[0:1, 1024:2048], AF.Sigmoid).then_inc(sSS, 1)

        nTT = bT + 5
        nVV = bV + 8
        nSS = bS + 3

        bT, bV, bS = nTT, nVV, nSS

        t_order = []
        for i in range(NEXP):
            if i < 8:
                t_order.append(("cb", i))
            t_order.append(("gate", i, 0))
            t_order.append(("up", i, 0))
            if i >= 1:
                for dn in range(16):
                    t_order.append(("down", i - 1, dn))
            for fc in range(1, NF):
                t_order.append(("gate", i, fc))
                t_order.append(("up", i, fc))
        for dn in range(16):
            t_order.append(("down", NEXP - 1, dn))
        ev6 = {k: bT + n + 1 for n, k in enumerate(t_order)}

        v_order = []
        for i in range(NEXP):
            if i < 8:
                v_order.append(("cbev", i))
            if i >= 1:
                for dn in range(16):
                    v_order.append(("dadd", i - 1, dn))
            for fc in range(NF):
                v_order.append(("h", i, fc))
        for dn in range(16):
            v_order.append(("dadd", NEXP - 1, dn))
        ev6v = {k: bV + n + 1 for n, k in enumerate(v_order)}

        def ev6_silu(i, fc):
            return bS + 11 * i + fc + 1

        with nc.Block("p6") as blk:
            @blk.gpsimd
            def _(g):
                w6 = [16, 16]
                w6d = 16
                for i in range(NEXP):
                    for fc in range(NF):
                        n = i * NF + fc
                        if n >= 2:
                            ip, fcp = divmod(n - 2, NF)
                            sem = sW6a if n % 2 == 0 else sW6b
                            g.wait_ge(sTT, ev6[("up", ip, fcp)])
                            g.wait_ge(sem, w6[n % 2])
                            g.dma_start(out=bf(R_WGU + (n % 2) * 4096, 4096),
                                        in_=wgu_r[i, fc, :, :]).then_inc(sem, 16)
                            w6[n % 2] += 16
                            wt_ev[("wgu", n)] = w6[n % 2]
                        if fc == 2 and i >= 1:
                            g.wait_ge(sTT, ev6[("down", i - 1, 15)])
                            g.wait_ge(sW6d, w6d)
                            g.dma_start(out=bf(R_WD, NF * 2048),
                                        in_=wd_r[i, :, :]).then_inc(sW6d, 16)
                            w6d += 16
                            wt_ev[("wd", i)] = w6d

            @blk.tensor
            def _(t):
                for item in t_order:
                    if item[0] == "cb":
                        i = item[1]
                        if i >= 1:
                            t.wait_ge(sSS, ev6_silu(i - 1, NF - 1))
                        for th in range(2):
                            ins = t.matmul(PB(th), sel_sb[:, i * 128:(i + 1) * 128],
                                           cbf[:, th * 512:(th + 1) * 512],
                                           start=True, stop=True)
                        ins.then_inc(sTT, 1)
                    elif item[0] == "gate":
                        _, i, fc = item
                        n = i * NF + fc
                        t.wait_ge(sW6a if n % 2 == 0 else sW6b, wt_ev[("wgu", n if n >= 2 else fc)])
                        if fc == 0:
                            if i < 8:
                                t.wait_ge(sVV, ev6v[("cbev", i)])
                            elif i >= 1:
                                t.wait_ge(sSS, ev6_silu(i - 1, NF - 1))
                        else:
                            t.wait_ge(sSS, ev6_silu(i, fc - 1))
                        for dc in range(ND):
                            for th in range(2):
                                ins = t.matmul(PB(th), bf(R_WGU + (n % 2) * 4096 + dc * 128, 128),
                                               bf(R_OO + dc * 1024 + th * 512, 512),
                                               start=(dc == 0), stop=(dc == ND - 1))
                        ins.then_inc(sTT, 1)
                    elif item[0] == "up":
                        _, i, fc = item
                        n = i * NF + fc
                        if fc >= 1:
                            t.wait_ge(sVV, ev6v[("h", i, fc - 1)])
                        elif i >= 1:
                            t.wait_ge(sVV, ev6v[("h", i - 1, NF - 1)])
                        for dc in range(ND):
                            for th in range(2):
                                ins = t.matmul(PB(2 + th), bf(R_WGU + (n % 2) * 4096 + 2048 + dc * 128, 128),
                                               bf(R_OO + dc * 1024 + th * 512, 512),
                                               start=(dc == 0), stop=(dc == ND - 1))
                        ins.then_inc(sTT, 1)
                    else:
                        _, i, dn = item
                        ts, hf = divmod(dn, 2)
                        gidx = i * 16 + dn
                        bk = 4 + 2 * (dn % 2)
                        if dn == 0:
                            t.wait_ge(sW6d, wt_ev[("wd", i)] if i >= 1 else wt_ev["wd0"])
                            t.wait_ge(sVV, ev6v[("h", i, NF - 1)])
                        if gidx >= 2:
                            ip, dnp = divmod(gidx - 2, 16)
                            t.wait_ge(sVV, ev6v[("dadd", ip, dnp)])
                        for fc in range(NF):
                            lhs = bf(R_H + fc * 1024 + ts * 128, 128)
                            t.matmul(PB(bk), lhs,
                                     bf(R_WD + fc * 2048 + hf * 1024, 512),
                                     start=(fc == 0), stop=(fc == NF - 1))
                            ins = t.matmul(PB(bk + 1), lhs,
                                           bf(R_WD + fc * 2048 + hf * 1024 + 512, 512),
                                           start=(fc == 0), stop=(fc == NF - 1))
                        ins.then_inc(sTT, 1)

            @blk.vector
            def _(v):
                for item in v_order:
                    if item[0] == "cbev":
                        i = item[1]
                        v.wait_ge(sTT, ev6[("cb", i)])
                        v.tensor_copy(bf(R_CB, 1024), PB(0, 2))
                        v.drain().then_inc(sVV, 1)
                    elif item[0] == "dadd":
                        _, i, dn = item
                        ts, hf = divmod(dn, 2)
                        bk = 4 + 2 * (dn % 2)
                        v.wait_ge(sTT, ev6[("down", i, dn)])
                        x1s = x1v[:, ts * 2048 + hf * 1024:ts * 2048 + (hf + 1) * 1024]
                        v.tensor_tensor(x1s, x1s, PB(bk, 2), ALU.add).then_inc(sVV, 1)
                    else:
                        _, i, fc = item
                        scaleb = bf(R_CB, 1024) if i < 8 else bf(R_GSB, 1024)
                        v.wait_ge(sTT, ev6[("up", i, fc)])
                        v.wait_ge(sSS, ev6_silu(i, fc))
                        hslice = bf(R_H + fc * 1024, 1024)
                        v.tensor_tensor(hslice, PB(2, 2), scaleb, ALU.mult)
                        v.drain()
                        v.tensor_tensor(hslice, hslice, bf(R_GATE, 1024), ALU.mult).then_inc(sVV, 1)

            @blk.scalar
            def _(s):
                for i in range(NEXP):
                    for fc in range(NF):
                        s.wait_ge(sTT, ev6[("gate", i, fc)])
                        if fc >= 1:
                            s.wait_ge(sVV, ev6v[("h", i, fc - 1)])
                        elif i >= 1:
                            s.wait_ge(sVV, ev6v[("h", i - 1, NF - 1)])
                        s.activation(bf(R_GATE, 1024), PB(0, 2), AF.Silu).then_inc(sSS, 1)

        nTT = bT + len(t_order)
        nVV = bV + len(v_order)
        nSS = bS + 11 * NEXP

        with nc.Block("p7") as blk:
            @blk.sync
            def _(sy):
                for qs in range(NQ):
                    sy.dma_start(out=out[qs * 128:(qs + 1) * 128, :],
                                 in_=x1v[:, qs * 2048:(qs + 1) * 2048]).then_inc(sST, 16)
                sy.wait_ge(sST, 16 * NQ)

    return nc


_NC_CACHE = [None]


def _prep_weights(inputs):
    f = lambda a: np.ascontiguousarray(a, dtype=np.float32)
    Wq = np.asarray(inputs["L0_attn_Wq"], np.float32)
    Wk = np.asarray(inputs["L0_attn_Wk"], np.float32)
    Wv = np.asarray(inputs["L0_attn_Wv"], np.float32)
    Wo = np.asarray(inputs["L0_attn_Wo"], np.float32)
    Wr = np.asarray(inputs["L0_ffn_Wr"], np.float32)
    Wsg = np.asarray(inputs["L0_ffn_Wsg"], np.float32)
    Wg = np.asarray(inputs["L0_ffn_Wg"], np.float32)
    Wu = np.asarray(inputs["L0_ffn_Wu"], np.float32)
    Wd = np.asarray(inputs["L0_ffn_Wd"], np.float32)
    Sg = np.asarray(inputs["L0_ffn_Sg"], np.float32)
    Su = np.asarray(inputs["L0_ffn_Su"], np.float32)
    Sd = np.asarray(inputs["L0_ffn_Sd"], np.float32)

    wq_r = f(Wq.reshape(16, 128, 4, 512).transpose(2, 1, 0, 3).reshape(4, 128, ND * 512))
    wk_r = f(Wk.reshape(16, 128, 512).transpose(1, 0, 2).reshape(128, ND * 512))
    wv_r = f(Wv.reshape(16, 128, 512).transpose(1, 0, 2).reshape(128, ND * 512))
    wo_r = f(Wo.reshape(2, 8, 128, 2048).transpose(0, 2, 1, 3).reshape(2, 128, 8 * 2048))
    wr_r = f(Wr.reshape(16, 128, 8).transpose(1, 0, 2).reshape(128, ND * 8))
    wsg_r = f(Wsg.reshape(16, 128, 1).transpose(1, 0, 2).reshape(128, ND))

    wg_e = Wg.reshape(8, 16, 128, 11, 128).transpose(0, 3, 2, 1, 4).reshape(8, 11, 128, 2048)
    sg_s = Sg.reshape(16, 128, 4, 11, 128).transpose(2, 3, 1, 0, 4).reshape(4, 11, 128, 2048)
    wg_r = np.concatenate([wg_e, sg_s], axis=0)
    wu_e = Wu.reshape(8, 16, 128, 11, 128).transpose(0, 3, 2, 1, 4).reshape(8, 11, 128, 2048)
    su_s = Su.reshape(16, 128, 4, 11, 128).transpose(2, 3, 1, 0, 4).reshape(4, 11, 128, 2048)
    wu_r = np.concatenate([wu_e, su_s], axis=0)
    wgu_r = f(np.concatenate([wg_r, wu_r], axis=3))
    wd_e = Wd.reshape(8, 11, 128, 2048).transpose(0, 2, 1, 3).reshape(8, 128, 11 * 2048)
    sd_s = Sd.reshape(4, 11, 128, 2048).transpose(0, 2, 1, 3).reshape(4, 128, 11 * 2048)
    wd_r = f(np.concatenate([wd_e, sd_s], axis=0))

    inv = 1.0 / (10000.0 ** (np.arange(0, 128, 2, dtype=np.float64) / 128.0))
    pos = np.arange(S, dtype=np.float64)
    fr = pos[:, None] * inv[None, :]
    emb = np.concatenate([fr, fr], axis=1)
    costab = np.cos(emb).astype(np.float32)
    sintab = np.sin(emb).astype(np.float32)

    identb = np.eye(128, dtype=ml_dtypes.bfloat16)
    identf = np.eye(128, dtype=np.float32)
    onesrow = np.ones((1, 128), dtype=ml_dtypes.bfloat16)
    onescol = np.ones((128, 1), dtype=ml_dtypes.bfloat16)
    sel3 = np.zeros((8, 8, 128), dtype=ml_dtypes.bfloat16)
    for e in range(8):
        sel3[e, e, :] = 1
    selm = np.ascontiguousarray(sel3.transpose(1, 0, 2).reshape(8, 8 * 128))
    consts = np.zeros((128, 2), dtype=np.float32)
    consts[:, 0] = 1e-5

    base = dict(wq_r=wq_r, wk_r=wk_r, wv_r=wv_r, wo_r=wo_r, wr_r=wr_r,
                wsg_r=wsg_r, wgu_r=wgu_r, wd_r=wd_r,
                identb=identb, identf=identf, onesrow=onesrow,
                onescol=onescol, sel=selm, consts=consts)
    return base, costab, sintab


def kernel(**inputs):
    x = np.asarray(inputs["L0_attn_x"], np.float32)
    base, costab, sintab = _prep_weights(inputs)
    if _NC_CACHE[0] is None:
        _NC_CACHE[0] = build_program()
    nc = _NC_CACHE[0]

    in_maps = []
    for c in range(8):
        b, q0 = c // 4, (c % 4) * 1024
        xs = np.roll(x[b], -q0, axis=0) if q0 else np.ascontiguousarray(x[b])
        ct = np.roll(costab, -q0, axis=0) if q0 else costab
        st = np.roll(sintab, -q0, axis=0) if q0 else sintab
        cos_rr = np.ascontiguousarray(ct.reshape(32, 128, 128).transpose(1, 0, 2).reshape(128, 4096))
        sin_rr = np.ascontiguousarray(st.reshape(32, 128, 128).transpose(1, 0, 2).reshape(128, 4096))
        m = dict(base)
        m["xseq"] = np.ascontiguousarray(xs)
        m["cos_r"] = cos_rr
        m["sin_r"] = sin_rr
        in_maps.append(m)

    res = run_bass_kernel_spmd(nc, in_maps, core_ids=list(range(8)))
    outp = np.empty((B, L, D), np.float32)
    for c in range(8):
        b, q0 = c // 4, (c % 4) * 1024
        outp[b, q0:q0 + 1024, :] = res.results[c]["out"]
    return outp
